# version 07
# baseline (speedup 1.0000x reference)
import math
import numpy as np
import concourse.bass as bass
import concourse.mybir as mybir
from concourse.bass_utils import run_bass_kernel_spmd

F32 = mybir.dt.float32
BF16 = mybir.dt.bfloat16
I32 = mybir.dt.int32
AF = mybir.ActivationFunctionType
ALU = mybir.AluOpType

PI = math.pi
TWO_PI = 2.0 * math.pi
INV_2PI = 1.0 / TWO_PI

D = 1024
NC_D = 8
DFF = 2048
DQ = 4096
DST = 64
T = 512
WD = 514
L = 32
NCHK = T // L
NPAIR = L // 2
QH = 256
LN_EPS = 1e-5


class Buf:
    __slots__ = ("name", "writers", "readers")

    def __init__(self, name):
        self.name = name
        self.writers = {}
        self.readers = {}


class Plan:
    def __init__(self):
        self.streams = {k: [] for k in ("T", "V", "A", "Q", "G")}
        self.cnt = {}
        self.marks = {k: {} for k in ("T", "V", "A", "Q", "G")}

    def step(self, stream, emit, reads=(), writes=(), semkey=None, dma=False):
        semkey = semkey or stream
        need = {}

        def add_need(d):
            for k, v in d.items():
                if k == semkey:
                    continue
                if need.get(k, 0) < v:
                    need[k] = v

        for b in reads:
            add_need(b.writers)
        for b in writes:
            add_need(b.writers)
            add_need(b.readers)
        m = self.marks[stream]
        waits = []
        for k, v in sorted(need.items()):
            if m.get(k, 0) < v:
                m[k] = v
                waits.append((k, v))
        inc = 16 if dma else 1
        self.cnt[semkey] = self.cnt.get(semkey, 0) + inc
        cnt = self.cnt[semkey]
        self.streams[stream].append((waits, emit, semkey, inc))
        for b in reads:
            b.readers[semkey] = cnt
        for b in writes:
            b.writers = {semkey: cnt}
            b.readers = {}
        return cnt


def build_program(n_seq, S, dbg_names=()):
    assert S % T == 0
    tiles_per_seq = S // T
    nc = bass.Bass("TRN2", target_bir_lowering=False,
                   dynamic_dma_scratch_size=16384,
                   detect_race_conditions=False)
    p = Plan()

    def din(name, shape):
        return nc.dram_tensor(name, list(shape), F32, kind="ExternalInput").ap()

    x_d = din("x", (n_seq, NC_D, 128, S))
    w1t_d = din("w1t", (D, D))
    v1t_d = din("v1t", (D, D))
    w2t_d = din("w2t", (D, D))
    bmt_d = din("bmt", (D, DST))
    cmt_d = din("cmt", (DST, D))
    a_d = din("amat", (DST, DST))
    ff1t_d = din("ff1t", (D, DFF))
    ff2t_d = din("ff2t", (DFF, D))
    W1_d = din("W1", (D, DQ))
    W2_d = din("W2", (DQ, D))
    convw_d = din("convw", (128, 24))
    g1_d = din("g1", (128, 8)); b1ln_d = din("b1ln", (128, 8))
    g2_d = din("g2", (128, 8)); b2ln_d = din("b2ln", (128, 8))
    w1b_d = din("w1b", (128, 8)); v1b_d = din("v1b", (128, 8))
    w2b_d = din("w2b", (128, 8)); ff2b_d = din("ff2b", (128, 8))
    ff1b_d = din("ff1b", (128, 16))
    qb1_d = din("qb1", (128, 32)); qb2_d = din("qb2", (128, 8))
    qth_d = din("qth", (128, 32))
    qthp_d = din("qthp_in", (128, 256))
    qwrp_d = din("qwrp_in", (128, 256))
    qwip_d = din("qwip_in", (128, 256))
    qwr_d = din("qwr", (128, 32)); qwi_d = din("qwi", (128, 32))
    qb1p_d = din("qb1p_in", (128, 256))
    out_d = nc.dram_tensor("out", [n_seq, NC_D, 128, S], F32,
                           kind="ExternalOutput").ap()

    W1s_d = nc.dram_tensor("W1s", [32, 128, NC_D, 128], BF16, kind="Internal").ap()
    f1s_d = nc.dram_tensor("f1s", [16, 128, NC_D, 128], BF16, kind="Internal").ap()
    W2s_d = nc.dram_tensor("W2s", [32, 128, D], BF16, kind="Internal").ap()
    f2s_d = nc.dram_tensor("f2s", [16, 128, D], BF16, kind="Internal").ap()

    sb = nc.alloc_sbuf_tensor
    w1t_bf = sb("w1t_bf", [128, NC_D * D], BF16)
    v1t_bf = sb("v1t_bf", [128, NC_D * D], BF16)
    w2t_bf = sb("w2t_bf", [128, NC_D * D], BF16)
    bmt_bf = sb("bmt_bf", [128, NC_D * DST], BF16)
    cmt_bf = sb("cmt_bf", [DST, D], BF16)
    a_f32 = sb("a_f32", [DST, DST], F32)
    a32t = sb("a32t", [DST, DST], F32)
    pbf = sb("pbf", [DST, 33 * DST], BF16)
    qpbf = sb("qpbf", [DST, 17 * DST], BF16)
    ppair = sb("ppair", [128, NPAIR * DST], BF16)
    iota64 = sb("iota64", [DST, DST], I32)
    convw_s = sb("convw_s", [128, 24], F32)
    g1_s = sb("g1_s", [128, 8], F32); b1ln_s = sb("b1ln_s", [128, 8], F32)
    g2_s = sb("g2_s", [128, 8], F32); b2ln_s = sb("b2ln_s", [128, 8], F32)
    w1b_s = sb("w1b_s", [128, 8], F32); v1b_s = sb("v1b_s", [128, 8], F32)
    w2b_s = sb("w2b_s", [128, 8], F32); ff2b_s = sb("ff2b_s", [128, 8], F32)
    ff1b_s = sb("ff1b_s", [128, 16], F32)
    qb1_s = sb("qb1_s", [128, 32], F32); qb2_s = sb("qb2_s", [128, 8], F32)
    qth_s = sb("qth_s", [128, 32], F32)
    qwr_s = sb("qwr_s", [128, 32], F32); qwi_s = sb("qwi_s", [128, 32], F32)
    qthm_s = sb("qthm_s", [128, 256], F32)
    qc_s = sb("qc_s", [128, 256], F32)
    qx_s = sb("qx_s", [128, 128], F32)
    cvec = sb("cvec", [128, 4], F32)
    ones128f = sb("ones128f", [1, 128], F32)
    oneskbf = sb("oneskbf", [128, 1], BF16)
    rows = sb("rows", [1, 2056], F32)
    xa = sb("xa", [128, NC_D * WD], F32)
    xnbf = sb("xnbf", [128, NC_D * WD], BF16)
    h1bf = sb("h1bf", [128, NC_D * WD], BF16)
    x1r = sb("x1r", [128, NC_D * WD], F32)
    xcbf = sb("xcbf", [128, NC_D * T], BF16)
    mid = sb("mid", [128, NC_D * T], F32)
    ffn = sb("ffn", [128, NC_D * T], F32)
    pf32 = nc.alloc_sbuf_tensor_at("pf32", [DST, 33 * DST], F32,
                                   offset=nc.lookup_mloc(mid).addr)
    qp32 = nc.alloc_sbuf_tensor_at("qp32", [DST, 17 * DST], F32,
                                   offset=nc.lookup_mloc(mid).addr + 8448)
    _mo = nc.lookup_mloc(mid).addr
    qthp = nc.alloc_sbuf_tensor_at("qthp", [128, 256], F32, offset=_mo + 12800)
    qwrp = nc.alloc_sbuf_tensor_at("qwrp", [128, 256], F32, offset=_mo + 13824)
    qwip = nc.alloc_sbuf_tensor_at("qwip", [128, 256], F32, offset=_mo + 14848)
    upad = sb("upad", [128, 1024], BF16)
    hs = sb("hs", [DST, NCHK + 1], BF16)
    esb = sb("esb", [DST, NCHK], BF16)
    hbf = sb("hbf", [DST, T], BF16)
    hidr = sb("hidr", [128, 3 * T], F32)
    mr = sb("mr", [128, 3 * T], F32)
    kr = sb("kr", [128, T], F32)
    MAGIC = float(1.5 * 2 ** 23)
    rw1 = sb("rw1", [128, 3 * NC_D * 128], BF16)
    rw2 = sb("rw2", [128, 6 * D], BF16)
    rf1 = sb("rf1", [128, 2 * NC_D * 128], BF16)
    rf2 = sb("rf2", [128, 2 * D], BF16)
    gelbf = nc.alloc_sbuf_tensor_at("gelbf", [128, 16 * T], BF16,
                                    offset=nc.lookup_mloc(x1r).addr)
    rbf = nc.alloc_sbuf_tensor_at("rbf", [128, 6 * T], BF16,
                                  offset=nc.lookup_mloc(h1bf).addr)

    SM = nc.alloc_psum_tensor("SM", [128, 1024], F32)
    BCA = nc.alloc_psum_tensor("BCA", [128, 1024], F32)
    BCB = nc.alloc_psum_tensor("BCB", [128, 1024], F32)
    MMX = nc.alloc_psum_tensor("MMX", [128, 1024], F32)

    B = {}
    for name in ["CONST", "W1TBF", "V1TBF", "W2TBF", "BMTBF", "CMTBF",
                 "AF32", "PF32", "PBF", "PPAIR", "IOTA",
                 "XA", "XNBF", "H1BF", "X1R", "XCBF", "MID",
                 "UPAD", "HS", "ESB", "HBF", "ROWS",
                 "SM0", "SM1", "BCA0", "BCA1", "BCB0", "BCB1", "MMX0", "MMX1",
                 "QTHM", "QC", "QX", "QPAD", "RES", "SCRALL"]:
        B[name] = Buf(name)
    FFNq = [Buf(f"FFN{i}") for i in range(4)]
    HIDq = [Buf(f"HID{i}") for i in range(3)]
    MRq = [Buf(f"MR{i}") for i in range(3)]
    KRq = [Buf(f"KR{i}") for i in range(2)]
    RBFq = [Buf(f"RBF{i}") for i in range(6)]
    RW1q = [Buf(f"RW1{i}") for i in range(3)]
    RW2q = [Buf(f"RW2{i}") for i in range(6)]
    RF1q = [Buf(f"RF1{i}") for i in range(4)]
    RF2q = [Buf(f"RF2{i}") for i in range(4)]
    SCRSLOT = [Buf(f"SCRS{i}") for i in range(4)]
    ACCB = (B["BCA0"], B["BCA0"], B["BCA1"], B["BCA1"],
            B["BCB0"], B["BCB0"], B["BCB1"], B["BCB1"])

    st = p.step

    def vstep(emit, reads=(), writes=()):
        st("V", emit, reads=reads, writes=writes)

    def astep(emit, reads=(), writes=()):
        st("A", emit, reads=reads, writes=writes)

    def tstep(emit, reads=(), writes=()):
        st("T", emit, reads=reads, writes=writes)

    def gdma(emit, reads=(), writes=(), semkey="GO"):
        st("G", emit, reads=reads, writes=writes, semkey=semkey, dma=True)

    def vv(*fns):
        def emit(e):
            last = None
            for f in fns:
                last = f(e)
            return last
        return emit

    def qload(dst_ap, src_ap, wbuf):
        st("Q", lambda e, d=dst_ap, s=src_ap: e.dma_start(d, s),
           writes=(B["CONST"],), semkey="QC", dma=True)

    qload(convw_s[:], convw_d, B["CONST"])
    for dst, src in [(g1_s, g1_d), (b1ln_s, b1ln_d), (g2_s, g2_d),
                     (b2ln_s, b2ln_d), (w1b_s, w1b_d), (v1b_s, v1b_d),
                     (w2b_s, w2b_d), (ff2b_s, ff2b_d), (ff1b_s, ff1b_d),
                     (qb1_s, qb1_d), (qb2_s, qb2_d), (qth_s, qth_d),
                     (qwr_s, qwr_d), (qwi_s, qwi_d)]:
        qload(dst[:], src, B["CONST"])
    qload(qthp[:], qthp_d, B["CONST"])
    qload(qwrp[:], qwrp_d, B["CONST"])
    qload(qwip[:], qwip_d, B["CONST"])
    qload(a_f32[:], a_d, B["CONST"])

    vstep(vv(lambda e: e.memset(cvec[:, 0:1], LN_EPS),
             lambda e: e.memset(cvec[:, 1:2], 0.1),
             lambda e: e.memset(cvec[:, 2:3], PI / 2),
             lambda e: e.memset(ones128f[:], 1.0),
             lambda e: e.memset(oneskbf[:], 1.0 / 1024.0),
             lambda e: e.memset(upad[:], 0.0)),
          writes=(B["CONST"], B["UPAD"]))

    def cast_res(dst_ap, src_ap, wbuf):
        st("G", lambda e, d=dst_ap, s=src_ap: e.dma_start(d, s),
           writes=(B["RES"],), semkey="GR", dma=True)

    cast_res(w1t_bf[:].rearrange("p (k d) -> p k d", k=NC_D),
             w1t_d.rearrange("(k p) d -> p k d", p=128), B["RES"])
    cast_res(v1t_bf[:].rearrange("p (k d) -> p k d", k=NC_D),
             v1t_d.rearrange("(k p) d -> p k d", p=128), B["RES"])
    cast_res(w2t_bf[:].rearrange("p (k d) -> p k d", k=NC_D),
             w2t_d.rearrange("(k p) d -> p k d", p=128), B["RES"])
    cast_res(bmt_bf[:].rearrange("p (k s) -> p k s", k=NC_D),
             bmt_d.rearrange("(k p) s -> p k s", p=128), B["RES"])
    cast_res(cmt_bf[:], cmt_d, B["RES"])

    st("G", lambda e: e.iota(iota64[:], pattern=[[1, DST]], base=0,
                             channel_multiplier=-1),
       writes=(B["IOTA"],), semkey="GC")
    vstep(vv(lambda e: e.tensor_scalar(pf32[:, 0:DST], iota64[:], 0.0, None,
                                       ALU.is_equal),
             lambda e: e.tensor_scalar(pbf[:, 0:DST], iota64[:], 0.0, None,
                                       ALU.is_equal)),
          reads=(B["IOTA"],), writes=(B["PF32"], B["PBF"]))

    for j in range(1, 33):
        def emit_mm(e, j=j):
            return e.matmul(SM[0:DST, 0:DST], a_f32[:],
                            pf32[:, (j - 1) * DST:j * DST],
                            start=True, stop=True)
        tstep(emit_mm, reads=(B["CONST"], B["PF32"]), writes=(B["SM0"],))
        def emit_cp(e, j=j):
            e.tensor_copy(pf32[:, j * DST:(j + 1) * DST], SM[0:DST, 0:DST])
            return e.tensor_copy(pbf[:, j * DST:(j + 1) * DST],
                                 SM[0:DST, 0:DST])
        vstep(emit_cp, reads=(B["SM0"],), writes=(B["PF32"], B["PBF"]))
    for jj in range(NPAIR):
        def emit_mm(e, jj=jj):
            return e.matmul(SM[64:128, 0:DST], a_f32[:],
                            pf32[:, (2 * jj) * DST:(2 * jj + 1) * DST],
                            start=True, stop=True)
        tstep(emit_mm, reads=(B["CONST"], B["PF32"]), writes=(B["SM0"],))
        def emit_cp(e, jj=jj):
            e.tensor_copy(ppair[0:64, jj * DST:(jj + 1) * DST],
                          pbf[:, (2 * jj) * DST:(2 * jj + 1) * DST])
            return e.tensor_copy(ppair[64:128, jj * DST:(jj + 1) * DST],
                                 SM[64:128, 0:DST])
        vstep(emit_cp, reads=(B["SM0"], B["PBF"]), writes=(B["PPAIR"],))

    def emit_tr(e):
        return e.transpose(SM[0:DST, 0:DST], pf32[:, 32 * DST:33 * DST],
                           pf32[:, 0:DST])
    tstep(emit_tr, reads=(B["PF32"],), writes=(B["SM0"],))
    vstep(lambda e: e.tensor_copy(a32t[:], SM[0:DST, 0:DST]),
          reads=(B["SM0"],), writes=(B["PF32"],))
    vstep(vv(lambda e: e.tensor_copy(qp32[:, 0:DST], pf32[:, 0:DST]),
             lambda e: e.tensor_copy(qpbf[:, 0:DST], pf32[:, 0:DST])),
          reads=(B["PF32"],), writes=(B["PBF"],))
    for k in range(1, 17):
        def emit_qmm(e, k=k):
            return e.matmul(SM[0:DST, 0:DST], a32t[:],
                            qp32[:, (k - 1) * DST:k * DST],
                            start=True, stop=True)
        tstep(emit_qmm, reads=(B["PF32"], B["PBF"]), writes=(B["SM0"],))
        def emit_qcp(e, k=k):
            e.tensor_copy(qp32[:, k * DST:(k + 1) * DST], SM[0:DST, 0:DST])
            return e.tensor_copy(qpbf[:, k * DST:(k + 1) * DST],
                                 SM[0:DST, 0:DST])
        vstep(emit_qcp, reads=(B["SM0"],), writes=(B["PBF"],))

    QP = 256
    k0 = kr[:, 0:QP]
    k1 = kr[:, QP:2 * QP]
    m0 = mr[:, 0:QP]
    m1 = mr[:, QP:2 * QP]
    h0 = hidr[:, 0:QP]
    vstep(vv(lambda e: e.tensor_scalar(k0, qthp[:], INV_2PI, MAGIC,
                                       ALU.mult, ALU.add),
             lambda e: e.tensor_scalar(k1, k0, MAGIC, 0.0,
                                       ALU.subtract, ALU.add),
             lambda e: e.scalar_tensor_tensor(qthm_s[:], k1, -TWO_PI,
                                              qthp[:], ALU.mult, ALU.add),
             lambda e: e.tensor_scalar(m0, qthm_s[:], PI / 2, 0.0,
                                       ALU.add, ALU.add),
             lambda e: e.tensor_scalar(k0, m0, INV_2PI, MAGIC,
                                       ALU.mult, ALU.add),
             lambda e: e.tensor_scalar(k1, k0, MAGIC, 0.0,
                                       ALU.subtract, ALU.add),
             lambda e: e.scalar_tensor_tensor(m1, k1, -TWO_PI, m0,
                                              ALU.mult, ALU.add)),
          reads=(B["CONST"],),
          writes=(B["QTHM"], KRq[0], MRq[0]))
    astep(vv(lambda e: e.activation(h0, qthm_s[:], AF.Sin),
             lambda e: e.activation(m0, m1, AF.Sin)),
          reads=(B["QTHM"], MRq[0]), writes=(HIDq[0], MRq[0]))
    qload(qthp[:], qb1p_d, B["CONST"])
    vstep(vv(lambda e: e.tensor_tensor(k0, m0, qwrp[:], ALU.mult),
             lambda e: e.tensor_tensor(k1, h0, qwip[:], ALU.mult),
             lambda e: e.tensor_tensor(k0, k0, k1, ALU.add),
             lambda e: e.tensor_tensor(k0, k0, qwip[:], ALU.mult),
             lambda e: e.tensor_scalar(qc_s[:], k0, 0.1, 0.0,
                                       ALU.mult, ALU.add),
             lambda e: e.tensor_tensor(k1, qc_s[:], h0, ALU.mult),
             lambda e: e.tensor_tensor(k0, qc_s[:], m0, ALU.mult),
             lambda e: e.tensor_scalar(m1, qthp[:], 0.1, 0.0,
                                       ALU.mult, ALU.add),
             lambda e: e.tensor_scalar(h0, m1, cvec[:, 2:3], 0.0,
                                       ALU.add, ALU.add)),
          reads=(B["CONST"], B["QTHM"], HIDq[0], MRq[0], KRq[0]),
          writes=(B["QC"], MRq[0], HIDq[0], KRq[0]))
    vstep(vv(lambda e: e.tensor_copy(qx_s[:, 0:32], mr[:, QP:QP + 32]),
             lambda e: e.tensor_copy(qx_s[:, 32:64], hidr[:, 0:32]),
             lambda e: e.tensor_copy(qx_s[:, 64:96], kr[:, QP:QP + 32]),
             lambda e: e.tensor_copy(qx_s[:, 96:128], kr[:, 0:32])),
          reads=(MRq[0], HIDq[0], KRq[0]),
          writes=(B["QX"],))
    if "boot" in dbg_names:
        bq = nc.dram_tensor("dbg_bqthm", [128, 32], F32, kind="ExternalOutput").ap()
        bs = nc.dram_tensor("dbg_bsth", [128, 32], F32, kind="ExternalOutput").ap()
        bc = nc.dram_tensor("dbg_bqc", [128, 32], F32, kind="ExternalOutput").ap()
        gdma(lambda e: e.dma_start(bq, qthm_s[:, 0:32]), reads=(B["QTHM"],))
        gdma(lambda e: e.dma_start(bs, hidr[:, 0:32]), reads=(HIDq[0],))
        gdma(lambda e: e.dma_start(bc, qc_s[:, 0:32]), reads=(B["QC"],))

    def emit_xload(seq, ti):
        t0 = ti * T
        first = (ti == 0)
        last_t = (ti == tiles_per_seq - 1)
        xsrc = x_d[seq].rearrange("c p t -> p c t")
        xa3 = xa[:].rearrange("p (c w) -> p c w", w=WD)
        if first and last_t:
            gdma(lambda e, d=xa3[:, :, 1:1 + S], s=xsrc[:, :, 0:S]:
                 e.dma_start(d, s), writes=(B["XA"],), semkey="GX")
            vstep(vv(lambda e: e.memset(xa3[:, :, 0:1], 0.0),
                     lambda e: e.memset(xa3[:, :, WD - 1:WD], 0.0)),
                  writes=(B["XA"],))
        elif first:
            gdma(lambda e, d=xa3[:, :, 1:WD], s=xsrc[:, :, 0:T + 1]:
                 e.dma_start(d, s), writes=(B["XA"],), semkey="GX")
            vstep(lambda e: e.memset(xa3[:, :, 0:1], 0.0),
                  writes=(B["XA"],))
        elif last_t:
            gdma(lambda e, d=xa3[:, :, 0:WD - 1],
                 s=xsrc[:, :, t0 - 1:t0 + T]: e.dma_start(d, s),
                 writes=(B["XA"],), semkey="GX")
            vstep(lambda e: e.memset(xa3[:, :, WD - 1:WD], 0.0),
                  writes=(B["XA"],))
        else:
            gdma(lambda e, d=xa3, s=xsrc[:, :, t0 - 1:t0 + T + 1]:
                 e.dma_start(d, s), writes=(B["XA"],), semkey="GX")

    emit_xload(0, 0)
    all_tiles = [(s2, t2) for s2 in range(n_seq) for t2 in range(tiles_per_seq)]

    scr_prio = ([("F1", m) for m in range(16)] +
                [("F2", q) for q in range(16)] +
                [("W1", m) for m in range(3)] +
                [("W2", q) for q in range(6)])
    scr_rest = ([("W1", m) for m in range(3, 32)] +
                [("W2", q) for q in range(6, 32)])
    scrb = {key: Buf("SCR" + key[0] + str(key[1]))
            for key in scr_prio + scr_rest}
    _scri = [0]

    def emit_scr(chunks):
        for kind, idx in chunks:
            i = _scri[0]
            _scri[0] += 1
            slot = i % 4
            stage = ffn[:, slot * 1024:(slot + 1) * 1024]
            stage3 = stage.rearrange("p (k m) -> p k m", k=NC_D)
            if kind == "W1":
                src = W1_d[:, idx * 128:(idx + 1) * 128].rearrange(
                    "(k p) m -> p k m", p=128)
                dst = W1s_d[idx]
                ssrc, sdst = stage3, stage3
            elif kind == "F1":
                src = ff1t_d[:, idx * 128:(idx + 1) * 128].rearrange(
                    "(k p) m -> p k m", p=128)
                dst = f1s_d[idx]
                ssrc, sdst = stage3, stage3
            elif kind == "W2":
                src = W2_d[idx * 128:(idx + 1) * 128, :]
                dst = W2s_d[idx]
                ssrc, sdst = stage, stage
            else:
                src = ff2t_d[idx * 128:(idx + 1) * 128, :]
                dst = f2s_d[idx]
                ssrc, sdst = stage, stage
            st("Q", lambda e, d=sdst, s=src: e.dma_start(d, s),
               writes=(FFNq[slot],), semkey=f"QS{slot}", dma=True)
            st("G", lambda e, d=dst, s=ssrc: e.dma_start(d, s),
               reads=(FFNq[slot],),
               writes=(SCRSLOT[slot], scrb[(kind, idx)]),
               semkey=f"GS{slot}", dma=True)

    emit_scr(scr_prio)

    class Ring:
        def __init__(self, name, tensor, slot_w, bufs, seq_fn, nuses,
                     nslots=4, dep_fn=None):
            self.name = name
            self.tensor = tensor
            self.slot_w = slot_w
            self.bufs = bufs
            self.seq_fn = seq_fn
            self.nuses = nuses
            self.nslots = nslots
            self.dep_fn = dep_fn
            self.filled = 0
            self.used = 0

        def fill_next(self):
            if self.filled >= self.nuses:
                return
            i = self.filled
            self.filled += 1
            slot = i % self.nslots
            dst = self.tensor[:, slot * self.slot_w:(slot + 1) * self.slot_w]
            src = self.seq_fn(i)
            st("Q", lambda e, d=dst, s=src: e.dma_start(d, s),
               reads=(self.dep_fn(i),), writes=(self.bufs[slot],),
               semkey=self.name + str(slot), dma=True)

        def use(self):
            i = self.used
            self.used += 1
            slot = i % self.nslots
            return (self.tensor[:, slot * self.slot_w:(slot + 1) * self.slot_w],
                    self.bufs[slot])

    total_tiles = n_seq * tiles_per_seq
    ring_w1 = Ring("qw1", rw1, NC_D * 128, RW1q,
                   lambda i: W1s_d[i % 32], total_tiles * 32, nslots=3,
                   dep_fn=lambda i: scrb[("W1", i % 32)])
    ring_w2 = Ring("qw2", rw2, D, RW2q,
                   lambda i: W2s_d[i % 32], total_tiles * 32, nslots=6,
                   dep_fn=lambda i: scrb[("W2", i % 32)])
    ring_f1 = Ring("qf1", rf1, NC_D * 128, RF1q,
                   lambda i: f1s_d[i % 16], total_tiles * 16, nslots=2,
                   dep_fn=lambda i: scrb[("F1", i % 16)])
    ring_f2 = Ring("qf2", rf2, D, RF2q,
                   lambda i: f2s_d[i % 16], total_tiles * 16, nslots=2,
                   dep_fn=lambda i: scrb[("F2", i % 16)])
    for r in (ring_w1, ring_w2, ring_f1, ring_f2):
        for _ in range(r.nslots):
            r.fill_next()
    emit_scr(scr_rest)

    def emit_ln(x_ap, xbuf, out_f32_ap, outbuf, width, gb=None,
                out_bf_ap=None, outbf_buf=None, extra_cast=None,
                skip_cast=False, cast_t=None, cast_buf=None, cast_stride=None,
                sq_t=None, sq_buf=None, sq_stride=None):
        stride = WD if width == WD else T
        halo = (width == WD)
        ct = xnbf if cast_t is None else cast_t
        cs = WD if cast_stride is None else cast_stride
        castbuf = B["XNBF"] if cast_buf is None else cast_buf
        sqt = h1bf if sq_t is None else sq_t
        sqs = WD if sq_stride is None else sq_stride
        sqbuf = B["H1BF"] if sq_buf is None else sq_buf
        if not skip_cast:
            def emit_cast(e):
                last = None
                for c in range(NC_D):
                    last = e.activation(
                        ct[:, c * cs:c * cs + width],
                        x_ap[:, c * stride:c * stride + width], AF.Identity)
                return last
            astep(emit_cast, reads=(xbuf,), writes=(castbuf,))

        def emit_sq(e):
            last = None
            for c in range(NC_D):
                last = e.activation(sqt[:, c * sqs:c * sqs + width],
                                    ct[:, c * cs:c * cs + width], AF.Square)
            return last
        astep(emit_sq, reads=(castbuf,), writes=(sqbuf,))

        def emit_mu(e):
            for k in range(NC_D):
                e.matmul(SM[0:1, 0:512], oneskbf[:],
                         ct[:, k * cs:k * cs + 512],
                         start=(k == 0), stop=(k == NC_D - 1))
            last = None
            for k in range(NC_D):
                last = e.matmul(SM[0:1, 512:1024], oneskbf[:],
                                sqt[:, k * sqs:k * sqs + 512],
                                start=(k == 0), stop=(k == NC_D - 1))
            return last
        tstep(emit_mu, reads=(castbuf, sqbuf, B["CONST"]),
              writes=(B["SM0"], B["SM1"]))
        if halo:
            def emit_halo(e):
                for k in range(NC_D):
                    e.matmul(BCB[0:1, 516:518], oneskbf[:],
                             ct[:, k * cs + 512:k * cs + 514],
                             start=(k == 0), stop=(k == NC_D - 1))
                last = None
                for k in range(NC_D):
                    last = e.matmul(BCB[0:1, 518:520], oneskbf[:],
                                    sqt[:, k * sqs + 512:k * sqs + 514],
                                    start=(k == 0), stop=(k == NC_D - 1))
                return last
            tstep(emit_halo, reads=(castbuf, sqbuf, B["CONST"]),
                  writes=(B["BCB1"],))

        def emit_rows1(e):
            e.tensor_copy(rows[0:1, 0:512], SM[0:1, 0:512])
            last = e.tensor_copy(rows[0:1, 1028:1540], SM[0:1, 512:1024])
            if halo:
                e.tensor_copy(rows[0:1, 512:514], BCB[0:1, 516:518])
                last = e.tensor_copy(rows[0:1, 1540:1542], BCB[0:1, 518:520])
            return last
        vstep(emit_rows1,
              reads=(B["SM0"], B["SM1"]) + ((B["BCB1"],) if halo else ()),
              writes=(B["ROWS"],))

        def emit_rows2(e):
            e.tensor_tensor(rows[0:1, 1542:1542 + width], rows[0:1, 0:width],
                            rows[0:1, 0:width], ALU.mult)
            return e.tensor_tensor(rows[0:1, 1028:1028 + width],
                                   rows[0:1, 1028:1028 + width],
                                   rows[0:1, 1542:1542 + width], ALU.subtract)
        vstep(emit_rows2, reads=(B["ROWS"],), writes=(B["ROWS"],))
        astep(lambda e: e.activation(rows[0:1, 1542:1542 + width],
                                     rows[0:1, 1028:1028 + width], AF.Sqrt,
                                     bias=cvec[0:1, 0:1]),
              reads=(B["ROWS"], B["CONST"]), writes=(B["ROWS"],))
        vstep(lambda e: e.reciprocal(rows[0:1, 514:514 + width],
                                     rows[0:1, 1542:1542 + width]),
              reads=(B["ROWS"],), writes=(B["ROWS"],))

        def emit_bc(e):
            e.matmul(SM[:, 0:512], ones128f[:], rows[0:1, 0:512],
                     start=True, stop=True)
            last = e.matmul(BCB[:, 0:512], ones128f[:], rows[0:1, 514:1026],
                            start=True, stop=True)
            if halo:
                e.matmul(SM[:, 512:514], ones128f[:], rows[0:1, 512:514],
                         start=True, stop=True)
                last = e.matmul(BCB[:, 512:514], ones128f[:],
                                rows[0:1, 1026:1028], start=True, stop=True)
            return last
        tstep(emit_bc, reads=(B["ROWS"], B["CONST"]),
              writes=(B["SM0"], B["BCB0"]) +
                     ((B["SM1"], B["BCB1"]) if halo else ()))

        rb = (B["SM0"], B["BCB0"]) + ((B["SM1"], B["BCB1"]) if halo else ())

        def emit_ap1(e):
            last = None
            for c in range(NC_D):
                last = e.tensor_tensor(
                    out_f32_ap[:, c * stride:c * stride + width],
                    x_ap[:, c * stride:c * stride + width],
                    SM[:, 0:width], ALU.subtract)
            return last
        vstep(emit_ap1, reads=(xbuf,) + rb, writes=(outbuf,))

        def emit_ap2(e):
            last = None
            for c in range(NC_D):
                last = e.tensor_tensor(
                    out_f32_ap[:, c * stride:c * stride + width],
                    out_f32_ap[:, c * stride:c * stride + width],
                    BCB[:, 0:width], ALU.mult)
            return last
        vstep(emit_ap2, reads=rb + (outbuf,), writes=(outbuf,))
        if gb is not None:
            g_ap, b_ap = gb
            def emit_gb(e):
                last = None
                for c in range(NC_D):
                    last = e.tensor_scalar(
                        out_bf_ap[:, c * WD:c * WD + width],
                        out_f32_ap[:, c * stride:c * stride + width],
                        g_ap[:, c:c + 1], b_ap[:, c:c + 1],
                        ALU.mult, ALU.add)
                return last
            vstep(emit_gb, reads=(outbuf, B["CONST"]), writes=(outbf_buf,))
        if extra_cast is not None:
            dst_bf_ap, dstbuf = extra_cast
            def emit_ec(e):
                last = None
                for c in range(NC_D):
                    last = e.activation(
                        dst_bf_ap[:, c * WD:c * WD + width],
                        out_f32_ap[:, c * stride:c * stride + width],
                        AF.Identity)
                return last
            astep(emit_ec, reads=(outbuf,), writes=(dstbuf,))

    dbg_specs = []
    dbg_done = [False]


    for tile_idx, (seq, ti) in enumerate(all_tiles):
        if True:
            t0 = ti * T
            first = (ti == 0)
            last_t = (ti == tiles_per_seq - 1)

            if tile_idx == 0:
                fdst = ffn[:].rearrange("p (c w) -> p c w", w=T)
                fsrc = x_d[0].rearrange("c p t -> p c t")[:, :, 0:T]
                gdma(lambda e, d=fdst, s=fsrc: e.dma_start(d, s),
                     writes=tuple(FFNq), semkey="GF")
            else:
                def emit_init(e):
                    last = None
                    for c in range(NC_D):
                        last = e.tensor_scalar(ffn[:, c * T:(c + 1) * T],
                                               xa[:, c * WD + 1:c * WD + 513],
                                               qb2_s[:, c:c + 1], None,
                                               ALU.add)
                    return last
                st("V", emit_init, reads=(B["XA"], B["CONST"]),
                   writes=tuple(FFNq))

            def emit_cast0(e):
                last = None
                for c in range(NC_D):
                    last = e.activation(xnbf[:, c * WD:c * WD + WD],
                                        xa[:, c * WD:c * WD + WD],
                                        AF.Identity)
                return last
            astep(emit_cast0, reads=(B["XA"],), writes=(B["XNBF"],))
            def emit_sq0(e):
                last = None
                for c in range(NC_D):
                    last = e.activation(h1bf[:, c * WD:c * WD + WD],
                                        xnbf[:, c * WD:c * WD + WD],
                                        AF.Square)
                return last
            astep(emit_sq0, reads=(B["XNBF"],), writes=(B["H1BF"],))
            def emit_mu0(e):
                for k in range(NC_D):
                    e.matmul(SM[0:1, 0:512], oneskbf[:],
                             xnbf[:, k * WD:k * WD + 512],
                             start=(k == 0), stop=(k == NC_D - 1))
                last = None
                for k in range(NC_D):
                    last = e.matmul(SM[0:1, 512:1024], oneskbf[:],
                                    h1bf[:, k * WD:k * WD + 512],
                                    start=(k == 0), stop=(k == NC_D - 1))
                return last
            tstep(emit_mu0, reads=(B["XNBF"], B["H1BF"], B["CONST"]),
                  writes=(B["SM0"], B["SM1"]))
            def emit_halo0(e):
                for k in range(NC_D):
                    e.matmul(BCB[0:1, 516:518], oneskbf[:],
                             xnbf[:, k * WD + 512:k * WD + 514],
                             start=(k == 0), stop=(k == NC_D - 1))
                last = None
                for k in range(NC_D):
                    last = e.matmul(BCB[0:1, 518:520], oneskbf[:],
                                    h1bf[:, k * WD + 512:k * WD + 514],
                                    start=(k == 0), stop=(k == NC_D - 1))
                return last
            tstep(emit_halo0, reads=(B["XNBF"], B["H1BF"], B["CONST"]),
                  writes=(B["BCB1"],))
            def emit_rows0(e):
                e.tensor_copy(rows[0:1, 0:512], SM[0:1, 0:512])
                e.tensor_copy(rows[0:1, 1028:1540], SM[0:1, 512:1024])
                e.tensor_copy(rows[0:1, 512:514], BCB[0:1, 516:518])
                return e.tensor_copy(rows[0:1, 1540:1542], BCB[0:1, 518:520])
            vstep(emit_rows0, reads=(B["SM0"], B["SM1"], B["BCB1"]),
                  writes=(B["ROWS"],))
            def emit_rows0b(e):
                e.tensor_tensor(rows[0:1, 1542:2056], rows[0:1, 0:514],
                                rows[0:1, 0:514], ALU.mult)
                return e.tensor_tensor(rows[0:1, 1028:1542],
                                       rows[0:1, 1028:1542],
                                       rows[0:1, 1542:2056], ALU.subtract)
            vstep(emit_rows0b, reads=(B["ROWS"],), writes=(B["ROWS"],))
            astep(lambda e: e.activation(rows[0:1, 1542:2056],
                                         rows[0:1, 1028:1542], AF.Sqrt,
                                         bias=cvec[0:1, 0:1]),
                  reads=(B["ROWS"], B["CONST"]), writes=(B["ROWS"],))
            vstep(lambda e: e.reciprocal(rows[0:1, 514:1028],
                                         rows[0:1, 1542:2056]),
                  reads=(B["ROWS"],), writes=(B["ROWS"],))
            def emit_r2a(e):
                e.tensor_tensor(rows[0:1, 1542:2056], rows[0:1, 1028:1542],
                                rows[0:1, 514:1028], ALU.mult)
                return e.tensor_tensor(rows[0:1, 1542:2056],
                                       rows[0:1, 1542:2056],
                                       rows[0:1, 514:1028], ALU.mult)
            vstep(emit_r2a, reads=(B["ROWS"],), writes=(B["ROWS"],))
            astep(lambda e: e.activation(rows[0:1, 1542:2056],
                                         rows[0:1, 1542:2056], AF.Sqrt,
                                         bias=cvec[0:1, 0:1]),
                  reads=(B["ROWS"], B["CONST"]), writes=(B["ROWS"],))
            def emit_r2b(e):
                e.reciprocal(rows[0:1, 1028:1542], rows[0:1, 1542:2056])
                return e.tensor_tensor(rows[0:1, 1028:1542],
                                       rows[0:1, 1028:1542],
                                       rows[0:1, 514:1028], ALU.mult)
            vstep(emit_r2b, reads=(B["ROWS"],), writes=(B["ROWS"],))
            def emit_bc0(e):
                e.matmul(SM[:, 0:512], ones128f[:], rows[0:1, 0:512],
                         start=True, stop=True)
                e.matmul(SM[:, 512:514], ones128f[:], rows[0:1, 512:514],
                         start=True, stop=True)
                e.matmul(BCB[:, 0:512], ones128f[:], rows[0:1, 514:1026],
                         start=True, stop=True)
                e.matmul(BCB[:, 512:514], ones128f[:],
                         rows[0:1, 1026:1028], start=True, stop=True)
                e.matmul(BCA[:, 0:512], ones128f[:], rows[0:1, 1028:1540],
                         start=True, stop=True)
                return e.matmul(BCA[:, 512:514], ones128f[:],
                                rows[0:1, 1540:1542], start=True, stop=True)
            tstep(emit_bc0, reads=(B["ROWS"], B["CONST"]),
                  writes=(B["SM0"], B["SM1"], B["BCB0"], B["BCB1"],
                          B["BCA0"], B["BCA1"]))
            def emit_ap1(e):
                last = None
                for c in range(NC_D):
                    last = e.tensor_tensor(x1r[:, c * WD:c * WD + WD],
                                           xa[:, c * WD:c * WD + WD],
                                           SM[:, 0:514], ALU.subtract)
                return last
            vstep(emit_ap1, reads=(B["XA"], B["SM0"], B["SM1"]),
                  writes=(B["X1R"],))
            def emit_ap2(e):
                last = None
                for c in range(NC_D):
                    last = e.tensor_tensor(xnbf[:, c * WD:c * WD + WD],
                                           x1r[:, c * WD:c * WD + WD],
                                           BCB[:, 0:514], ALU.mult)
                return last
            vstep(emit_ap2, reads=(B["X1R"], B["BCB0"], B["BCB1"]),
                  writes=(B["XNBF"],))
            def emit_l1a(e):
                last = None
                for c in range(NC_D):
                    last = e.tensor_tensor(x1r[:, c * WD:c * WD + WD],
                                           x1r[:, c * WD:c * WD + WD],
                                           BCA[:, 0:514], ALU.mult)
                return last
            vstep(emit_l1a, reads=(B["X1R"], B["BCA0"], B["BCA1"]),
                  writes=(B["X1R"],))
            def emit_l1b(e):
                last = None
                for c in range(NC_D):
                    last = e.tensor_scalar(h1bf[:, c * WD:c * WD + WD],
                                           x1r[:, c * WD:c * WD + WD],
                                           g1_s[:, c:c + 1],
                                           b1ln_s[:, c:c + 1],
                                           ALU.mult, ALU.add)
                return last
            vstep(emit_l1b, reads=(B["X1R"], B["CONST"]),
                  writes=(B["H1BF"],))

            def phase_dump(tag, tsr, bb, shp, dt):
                if seq == 0 and ti == 0 and ("p_" + tag) in dbg_names:
                    dd = nc.dram_tensor("dbg_p_" + tag, shp, dt,
                                        kind="ExternalOutput").ap()
                    gdma(lambda e, d=dd, s=tsr: e.dma_start(d, s[:]),
                         reads=(bb,))
            phase_dump("h1", h1bf, B["H1BF"], [128, NC_D * WD], BF16)
            phase_dump("w1t", w1t_bf, B["RES"], [128, NC_D * D], BF16)
            phase_dump("v1t", v1t_bf, B["RES"], [128, NC_D * D], BF16)

            for c in range(NC_D):
                rot = MMX if (c % 2 == 0) else BCA
                rb0 = B["MMX0"] if (c % 2 == 0) else B["BCA0"]
                rb1 = B["MMX1"] if (c % 2 == 0) else B["BCA1"]
                def emit_mm(e, c=c, rot=rot):
                    for k in range(NC_D):
                        e.matmul(rot[:, 0:512],
                                 w1t_bf[:, k * D + c * 128:k * D + c * 128 + 128],
                                 h1bf[:, k * WD:k * WD + 512],
                                 start=(k == 0), stop=(k == NC_D - 1))
                    last = None
                    for k in range(NC_D):
                        last = e.matmul(rot[:, 512:514],
                                 w1t_bf[:, k * D + c * 128:k * D + c * 128 + 128],
                                 h1bf[:, k * WD + 512:k * WD + 514],
                                 start=(k == 0), stop=(k == NC_D - 1))
                    return last
                tstep(emit_mm, reads=(B["H1BF"], B["RES"]), writes=(rb0, rb1))
                def emit_ev(e, c=c, rot=rot):
                    return e.activation(x1r[:, c * WD:c * WD + WD],
                                        rot[:, 0:514], AF.Identity,
                                        bias=w1b_s[:, c:c + 1])
                astep(emit_ev, reads=(rb0, rb1, B["CONST"]),
                      writes=(B["X1R"],))

            x13 = x1r[:].rearrange("p (c w) -> p c w", w=WD)
            if first:
                vstep(lambda e: e.memset(x13[:, :, 0:1], 0.0),
                      writes=(B["X1R"],))
            if last_t:
                vstep(lambda e: e.memset(x13[:, :, WD - 1:WD], 0.0),
                      writes=(B["X1R"],))
            phase_dump("x1", x1r, B["X1R"], [128, NC_D * WD], F32)

            for c in range(NC_D):
                def emit_conv(e, c=c):
                    e.tensor_scalar(mid[:, c * T:(c + 1) * T],
                                    x1r[:, c * WD:c * WD + 512],
                                    convw_s[:, 3 * c:3 * c + 1], None, ALU.mult)
                    e.scalar_tensor_tensor(mid[:, c * T:(c + 1) * T],
                                           x1r[:, c * WD + 1:c * WD + 513],
                                           convw_s[:, 3 * c + 1:3 * c + 2],
                                           mid[:, c * T:(c + 1) * T],
                                           ALU.mult, ALU.add)
                    return e.scalar_tensor_tensor(
                        mid[:, c * T:(c + 1) * T],
                        x1r[:, c * WD + 2:c * WD + 514],
                        convw_s[:, 3 * c + 2:3 * c + 3],
                        mid[:, c * T:(c + 1) * T], ALU.mult, ALU.add)
                vstep(emit_conv, reads=(B["X1R"], B["CONST"]),
                      writes=(B["MID"],))
                def emit_silu(e, c=c):
                    return e.activation(xcbf[:, c * T:(c + 1) * T],
                                        mid[:, c * T:(c + 1) * T], AF.Silu)
                astep(emit_silu, reads=(B["MID"],), writes=(B["XCBF"],))
            phase_dump("xc", xcbf, B["XCBF"], [128, NC_D * T], BF16)

            VPS = (SM, SM, BCA, BCA, BCB, BCB, MMX, MMX)
            VPSB = (B["SM0"], B["SM1"], B["BCA0"], B["BCA1"],
                    B["BCB0"], B["BCB1"], B["MMX0"], B["MMX1"])
            for c in range(NC_D):
                acc = VPS[c][:, (c % 2) * 512:(c % 2) * 512 + 512]
                def emit_mm(e, c=c, acc=acc):
                    last = None
                    for k in range(NC_D):
                        last = e.matmul(
                            acc,
                            v1t_bf[:, k * D + c * 128:k * D + c * 128 + 128],
                            h1bf[:, k * WD + 1:k * WD + 513],
                            start=(k == 0), stop=(k == NC_D - 1))
                    return last
                tstep(emit_mm, reads=(B["H1BF"], B["RES"]), writes=(VPSB[c],))
            for c in range(NC_D):
                acc = VPS[c][:, (c % 2) * 512:(c % 2) * 512 + 512]
                def emit_ev(e, c=c, acc=acc):
                    return e.activation(h1bf[:, c * WD:c * WD + 512],
                                        acc, AF.Silu,
                                        bias=v1b_s[:, c:c + 1])
                astep(emit_ev, reads=(VPSB[c], B["CONST"]),
                      writes=(B["H1BF"],))
            phase_dump("v", h1bf, B["H1BF"], [128, NC_D * WD], BF16)

            def emit_umm(e):
                for k in range(NC_D):
                    e.matmul(BCB[0:64, 0:512],
                             bmt_bf[:, k * DST:(k + 1) * DST],
                             xcbf[:, k * T:(k + 1) * T],
                             start=(k == 0), stop=(k == NC_D - 1))
                last = None
                for k in range(NC_D):
                    last = e.matmul(BCB[64:128, 0:512],
                                    bmt_bf[:, k * DST:(k + 1) * DST],
                                    xcbf[:, k * T:(k + 1) * T],
                                    start=(k == 0), stop=(k == NC_D - 1))
                return last
            tstep(emit_umm, reads=(B["XCBF"], B["RES"]), writes=(B["BCB0"],))
            up_lo = upad[0:64, :].rearrange("p (c w) -> p c w", w=64)[:, :, 0:32]
            up_hi = upad[64:128, :].rearrange("p (c w) -> p c w", w=64)[:, :, 1:33]
            src_lo = BCB[0:64, 0:512].rearrange("p (c w) -> p c w", w=32)
            src_hi = BCB[64:128, 0:512].rearrange("p (c w) -> p c w", w=32)
            vstep(vv(lambda e: e.tensor_copy(up_lo, src_lo),
                     lambda e: e.tensor_copy(up_hi, src_hi)),
                  reads=(B["BCB0"],), writes=(B["UPAD"],))

            def emit_shifts(e):
                last = None
                for j in range(NPAIR):
                    lhs = ppair[:, j * DST:(j + 1) * DST]
                    e.matmul(MMX[0:64, 2 * j:512], lhs,
                             upad[:, 0:512 - 2 * j],
                             start=(j == 0), stop=False, skip_group_check=True)
                    last = e.matmul(MMX[0:64, 512:1024], lhs,
                                    upad[:, 512 - 2 * j:1024 - 2 * j],
                                    start=(j == 0), stop=False,
                                    skip_group_check=True)
                return last
            tstep(emit_shifts, reads=(B["UPAD"], B["PPAIR"]),
                  writes=(B["MMX0"], B["MMX1"]))

            esrc = MMX[0:64, :].rearrange("p (c w) -> p c w", w=64)[:, :, 31:32]
            edst = esb[:, :].rearrange("p (c b) -> p c b", b=1)
            vstep(lambda e: e.tensor_copy(edst, esrc),
                  reads=(B["MMX0"], B["MMX1"]), writes=(B["ESB"],))

            if first:
                vstep(lambda e: e.memset(hs[:, 0:1], 0.0), writes=(B["HS"],))
            else:
                vstep(lambda e: e.tensor_copy(hs[:, 0:1],
                                              hs[:, NCHK:NCHK + 1]),
                      reads=(B["HS"],), writes=(B["HS"],))

            def emit_hsmm(e):
                for k in range(NCHK):
                    e.matmul(BCB[0:DST, k + 1:NCHK + 1],
                             qpbf[:, k * DST:(k + 1) * DST],
                             esb[:, 0:NCHK - k],
                             start=(k == 0), stop=False,
                             skip_group_check=True)
                last = None
                for c2 in range(1, NCHK + 1):
                    last = e.matmul(BCB[0:DST, c2:c2 + 1],
                                    qpbf[:, c2 * DST:(c2 + 1) * DST],
                                    hs[:, 0:1], start=False,
                                    stop=(c2 == NCHK), skip_group_check=True)
                return last
            tstep(emit_hsmm, reads=(B["ESB"], B["HS"], B["PBF"]),
                  writes=(B["BCB0"],))
            vstep(lambda e: e.tensor_copy(hs[:, 1:NCHK + 1],
                                          BCB[0:DST, 1:NCHK + 1]),
                  reads=(B["BCB0"],), writes=(B["HS"],))

            def emit_g(e):
                last = None
                for q in range(L):
                    lhs = pbf[:, (q + 1) * DST:(q + 2) * DST]
                    e.matmul(MMX[0:64, q:512:64], lhs, hs[:, 0:8],
                             start=False, stop=False, skip_group_check=True)
                    last = e.matmul(MMX[0:64, 512 + q:1024:64], lhs,
                                    hs[:, 8:16], start=False,
                                    stop=(q == L - 1), skip_group_check=True)
                return last
            tstep(emit_g, reads=(B["HS"], B["PBF"]),
                  writes=(B["MMX0"], B["MMX1"]))
            hsrc = MMX[0:64, :].rearrange("p (c w) -> p c w", w=64)[:, :, 0:32]
            hdst = hbf[:, :].rearrange("p (c w) -> p c w", w=32)
            vstep(lambda e: e.tensor_copy(hdst, hsrc),
                  reads=(B["MMX0"], B["MMX1"]), writes=(B["HBF"],))

            for c in range(NC_D):
                rot_ap = MMX[:, 0:512] if (c % 2 == 0) else MMX[:, 512:1024]
                rotb = B["MMX0"] if (c % 2 == 0) else B["MMX1"]
                def emit_y(e, c=c, rot_ap=rot_ap):
                    return e.matmul(rot_ap, cmt_bf[:, c * 128:(c + 1) * 128],
                                    hbf[:], start=True, stop=True)
                tstep(emit_y, reads=(B["HBF"], B["RES"]), writes=(rotb,))
                def emit_gate(e, c=c, rot_ap=rot_ap):
                    return e.tensor_tensor(xcbf[:, c * T:(c + 1) * T], rot_ap,
                                           h1bf[:, c * WD:c * WD + 512],
                                           ALU.mult)
                vstep(emit_gate, reads=(rotb, B["H1BF"]), writes=(B["XCBF"],))

            for c in range(NC_D):
                rot_ap = MMX[:, 0:512] if (c % 2 == 0) else MMX[:, 512:1024]
                rotb = B["MMX0"] if (c % 2 == 0) else B["MMX1"]
                def emit_mm(e, c=c, rot_ap=rot_ap):
                    last = None
                    for k in range(NC_D):
                        last = e.matmul(
                            rot_ap,
                            w2t_bf[:, k * D + c * 128:k * D + c * 128 + 128],
                            xcbf[:, k * T:(k + 1) * T],
                            start=(k == 0), stop=(k == NC_D - 1))
                    return last
                tstep(emit_mm, reads=(B["XCBF"], B["RES"]), writes=(rotb,))
                def emit_ev(e, c=c, rot_ap=rot_ap):
                    return e.scalar_tensor_tensor(
                        mid[:, c * T:(c + 1) * T], rot_ap, w2b_s[:, c:c + 1],
                        xnbf[:, c * WD + 1:c * WD + 513], ALU.add, ALU.add)
                vstep(emit_ev, reads=(rotb, B["CONST"], B["XNBF"]),
                      writes=(B["MID"],))
                def emit_mc(e, c=c):
                    return e.activation(h1bf[:, c * WD:c * WD + 512],
                                        mid[:, c * T:(c + 1) * T],
                                        AF.Identity)
                astep(emit_mc, reads=(B["MID"],), writes=(B["H1BF"],))

            if tile_idx + 1 < len(all_tiles):
                emit_xload(*all_tiles[tile_idx + 1])

            emit_ln(mid[:], B["MID"], x1r[:], B["X1R"], T,
                    gb=(g2_s, b2ln_s), out_bf_ap=h1bf[:], outbf_buf=B["H1BF"],
                    skip_cast=True, cast_t=h1bf, cast_buf=B["H1BF"],
                    cast_stride=WD, sq_t=xcbf, sq_buf=B["XCBF"], sq_stride=T)


            for m in range(16):
                f1slot, f1buf = ring_f1.use()
                rot_ap = MMX[:, 0:512] if (m % 2 == 0) else MMX[:, 512:1024]
                rotb = B["MMX0"] if (m % 2 == 0) else B["MMX1"]
                def emit_mm(e, f1slot=f1slot, rot_ap=rot_ap):
                    last = None
                    for k in range(NC_D):
                        last = e.matmul(
                            rot_ap, f1slot[:, k * 128:(k + 1) * 128],
                            h1bf[:, k * WD:k * WD + 512],
                            start=(k == 0), stop=(k == NC_D - 1))
                    return last
                tstep(emit_mm, reads=(B["H1BF"], f1buf), writes=(rotb,))
                ring_f1.fill_next()
                def emit_ev(e, m=m, rot_ap=rot_ap):
                    return e.activation(gelbf[:, m * T:(m + 1) * T], rot_ap,
                                        AF.Gelu, bias=ff1b_s[:, m:m + 1])
                astep(emit_ev, reads=(rotb, B["CONST"]), writes=(B["X1R"],))

            PS4 = (SM, BCA, BCB, MMX)
            PSB = (B["SM0"], B["SM1"], B["BCA0"], B["BCA1"],
                   B["BCB0"], B["BCB1"], B["MMX0"], B["MMX1"])
            for k in range(16):
                f2slot, f2buf = ring_f2.use()
                def emit_ff2(e, k=k, f2slot=f2slot):
                    last = None
                    for o in range(8):
                        for th in range(2):
                            r = o * 2 + th
                            acc = PS4[r // 4]
                            col = (r % 4) * 256
                            last = e.matmul(
                                acc[:, col:col + 256],
                                f2slot[:, o * 128:(o + 1) * 128],
                                gelbf[:, k * T + th * QH:k * T + th * QH + QH],
                                start=(k == 0 and r % 2 == 0), stop=False,
                                skip_group_check=True)
                    return last
                tstep(emit_ff2, reads=(f2buf, B["X1R"]), writes=PSB)
                ring_f2.fill_next()
            for r in list(range(12, 16)) + list(range(12)):
                    o = r // 2
                    th = r % 2
                    acc = PS4[r // 4]
                    col = (r % 4) * 256
                    dst = mid[:, o * T + th * QH:o * T + th * QH + QH]
                    bankbuf = PSB[(r // 4) * 2 + (r % 4) // 2]
                    def emit_ev(e, acc=acc, col=col, dst=dst, o=o):
                        return e.scalar_tensor_tensor(
                            dst, acc[:, col:col + 256], ff2b_s[:, o:o + 1],
                            dst, ALU.add, ALU.add)
                    vstep(emit_ev, reads=(bankbuf, B["CONST"], B["MID"]),
                          writes=(B["MID"],))

            def emit_am(e, tile_idx=tile_idx):
                last = None
                for c in range(NC_D):
                    if tile_idx == 0:
                        last = e.scalar_tensor_tensor(
                            ffn[:, c * T:(c + 1) * T],
                            mid[:, c * T:(c + 1) * T], qb2_s[:, c:c + 1],
                            ffn[:, c * T:(c + 1) * T], ALU.add, ALU.add)
                    else:
                        last = e.tensor_tensor(ffn[:, c * T:(c + 1) * T],
                                               ffn[:, c * T:(c + 1) * T],
                                               mid[:, c * T:(c + 1) * T],
                                               ALU.add)
                return last
            vstep(emit_am, reads=(B["MID"], B["CONST"]) + tuple(FFNq),
                  writes=tuple(FFNq))

            def q_w1(c):
                w1slot, w1buf = ring_w1.use()
                rot_ap = MMX[:, 0:512] if (c % 2 == 0) else MMX[:, 512:1024]
                rotb = B["MMX0"] if (c % 2 == 0) else B["MMX1"]
                def emit_w1(e, w1slot=w1slot, rot_ap=rot_ap):
                    last = None
                    for k in range(NC_D):
                        last = e.matmul(
                            rot_ap, w1slot[:, k * 128:(k + 1) * 128],
                            xnbf[:, k * WD + 1:k * WD + 513],
                            start=(k == 0), stop=(k == NC_D - 1))
                    return last
                tstep(emit_w1, reads=(B["XNBF"], w1buf), writes=(rotb,))
                ring_w1.fill_next()
                hs3 = c % 3
                sr_ap = mr[:, (c % 2) * T:(c % 2) * T + T]
                cr_ap = mr[:, 2 * T:3 * T] if (c % 2 == 0) else kr[:]
                srb = MRq[c % 2]
                crb = MRq[2] if (c % 2 == 0) else KRq[0]
                def emit_ev(e, rot_ap=rot_ap, hs3=hs3, c=c,
                            sr_ap=sr_ap, cr_ap=cr_ap):
                    e.activation(hidr[:, hs3 * T:(hs3 + 1) * T],
                                 rot_ap, AF.Identity,
                                 bias=qb1_s[:, c:c + 1])
                    e.activation(sr_ap, rot_ap, AF.Sin,
                                 bias=qx_s[:, c:c + 1], scale=0.1)
                    return e.activation(cr_ap, rot_ap, AF.Sin,
                                        bias=qx_s[:, 32 + c:32 + c + 1],
                                        scale=0.1)
                astep(emit_ev, reads=(rotb, B["CONST"], B["QX"]),
                      writes=(HIDq[hs3], srb, crb))

            def q_mix(c):
                hs3 = c % 3
                hid_ap = hidr[:, hs3 * T:(hs3 + 1) * T]
                sr_ap = mr[:, (c % 2) * T:(c % 2) * T + T]
                cr_ap = mr[:, 2 * T:3 * T] if (c % 2 == 0) else kr[:]
                srb = MRq[c % 2]
                crb = MRq[2] if (c % 2 == 0) else KRq[0]
                def emit_mix(e, hid_ap=hid_ap, sr_ap=sr_ap, cr_ap=cr_ap, c=c):
                    e.scalar_tensor_tensor(hid_ap, cr_ap,
                                           qx_s[:, 64 + c:64 + c + 1],
                                           hid_ap, ALU.mult, ALU.add)
                    return e.scalar_tensor_tensor(hid_ap, sr_ap,
                                                  qx_s[:, 96 + c:96 + c + 1],
                                                  hid_ap, ALU.mult, ALU.add)
                vstep(emit_mix, reads=(srb, crb, HIDq[hs3], B["QX"]),
                      writes=(HIDq[hs3],))

            def q_relu(c):
                hs3 = c % 3
                rs = c % 6
                hid_ap = hidr[:, hs3 * T:(hs3 + 1) * T]
                r_ap = rbf[:, rs * T:(rs + 1) * T]
                astep(lambda e, hid_ap=hid_ap, r_ap=r_ap:
                      e.activation(r_ap, hid_ap, AF.Relu),
                      reads=(HIDq[hs3],),
                      writes=(RBFq[rs], B["H1BF"]))

            def q_w2group(g):
                slots = []
                for kk in range(4):
                    w2slot, w2buf = ring_w2.use()
                    slots.append((w2slot, w2buf))
                GPS = (BCA[:, 0:512], BCA[:, 512:1024],
                       BCB[:, 0:512], BCB[:, 512:1024])
                GPSB = (B["BCA0"], B["BCA1"], B["BCB0"], B["BCB1"])
                for o in range(8):
                    acc_ap = GPS[o % 4]
                    accb = GPSB[o % 4]
                    def emit_grp(e, g=g, o=o, acc_ap=acc_ap, slots=slots):
                        last = None
                        for kk in range(4):
                            c2 = 4 * g + kk
                            last = e.matmul(
                                acc_ap,
                                slots[kk][0][:, o * 128:(o + 1) * 128],
                                rbf[:, (c2 % 6) * T:(c2 % 6 + 1) * T],
                                start=(kk == 0), stop=(kk == 3))
                        return last
                    rb = tuple(RBFq[(4 * g + kk) % 6] for kk in range(4))
                    tstep(emit_grp,
                          reads=rb + (B["H1BF"],) + tuple(s[1] for s in slots),
                          writes=(accb,))
                    dst = ffn[:, o * T:(o + 1) * T]
                    mdst = mid[:, o * T:(o + 1) * T]
                    if g == 7:
                        def emit_add(e, acc_ap=acc_ap, dst=dst, mdst=mdst):
                            return e.scalar_tensor_tensor(
                                mdst, dst, 1.0, acc_ap, ALU.mult, ALU.add)
                        vstep(emit_add, reads=(accb, FFNq[o // 2]),
                              writes=(B["MID"],))
                    else:
                        def emit_add(e, acc_ap=acc_ap, dst=dst):
                            return e.tensor_tensor(dst, dst, acc_ap, ALU.add)
                        vstep(emit_add, reads=(accb, FFNq[o // 2]),
                              writes=(FFNq[o // 2],))
                for kk in range(4):
                    ring_w2.fill_next()

            for c in range(32):
                q_w1(c)
                if c >= 1:
                    q_mix(c - 1)
                    q_relu(c - 1)
                if c >= 5 and (c - 5) % 4 == 0:
                    q_w2group((c - 5) // 4)
            q_mix(31)
            q_relu(31)
            q_w2group(7)

            odst = out_d[seq].rearrange("c p t -> p c t")[:, :, t0:t0 + T]
            osrc = mid[:].rearrange("p (c w) -> p c w", w=T)
            gdma(lambda e, d=odst, s=osrc: e.dma_start(d, s),
                 reads=(B["MID"],), writes=())

            if dbg_names and not dbg_done[0]:
                dbg_done[0] = True
                cat = {"xa": (xa, B["XA"], [128, NC_D * WD], F32),
                       "xnbf": (xnbf, B["XNBF"], [128, NC_D * WD], BF16),
                       "h1bf": (h1bf, B["H1BF"], [128, NC_D * WD], BF16),
                       "x1r": (x1r, B["X1R"], [128, NC_D * WD], F32),
                       "xcbf": (xcbf, B["XCBF"], [128, NC_D * T], BF16),
                       "mid": (mid, B["MID"], [128, NC_D * T], F32),
                       "upad": (upad, B["UPAD"], [128, 1024], BF16),
                       "hs": (hs, B["HS"], [DST, NCHK + 1], BF16),
                       "esb": (esb, B["ESB"], [DST, NCHK], F32),
                       "hbf": (hbf, B["HBF"], [DST, T], BF16),
                       "qthm": (qthm_s, B["QTHM"], [128, 256], F32),
                       "qths": (qth_s, B["CONST"], [128, 32], F32),
                       "g1s": (g1_s, B["CONST"], [128, 8], F32),
                       "pbf": (pbf, B["PBF"], [64, 33 * 64], BF16),
                       "qc": (qc_s, B["QC"], [128, 256], F32),
                       "qx": (qx_s, B["QX"], [128, 128], F32),
                       "ffn": (ffn, FFNq[0], [128, NC_D * T], F32),
                       "rbf": (rbf, RBFq[0], [128, 6 * T], BF16),
                       "hidr": (hidr, HIDq[0], [128, 3 * T], F32)}
                for nm in dbg_names:
                    if nm not in cat:
                        continue
                    tsr, bb, shp, dt = cat[nm]
                    dd = nc.dram_tensor("dbg_" + nm, shp, dt,
                                        kind="ExternalOutput").ap()
                    dbg_specs.append(nm)
                    gdma(lambda e, d=dd, s=tsr: e.dma_start(d, s[:]),
                         reads=(bb,), writes=())

    sems = {k: nc.alloc_semaphore("sem" + k) for k in p.cnt.keys()}
    qkeys = [k for k in p.cnt if k.startswith("Q") or k.startswith("qw")
             or k.startswith("qf") or k == "QC"]
    gkeys = [k for k in p.cnt if k in ("GO", "GX", "GF", "GR")
             or k.startswith("GS")]

    def runner(skey):
        steps = p.streams[skey]
        def run(eng):
            for waits, emit, semkey, inc in steps:
                for (k, v) in waits:
                    eng.wait_ge(sems[k], v)
                last = emit(eng)
                last.then_inc(sems[semkey], inc)
            if skey == "G":
                for k in gkeys:
                    eng.wait_ge(sems[k], p.cnt[k])
            if skey == "Q":
                for k in qkeys:
                    eng.wait_ge(sems[k], p.cnt[k])
        return run

    with nc.Block() as blk:
        blk.tensor(runner("T"))
        blk.vector(runner("V"))
        blk.scalar(runner("A"))
        blk.sync(runner("Q"))
        blk.gpsimd(runner("G"))

    return nc, dbg_specs


def _prep_weights(iv):
    def pad256(a):
        z = np.zeros((128, 256), np.float32)
        z[:, :a.shape[1]] = a
        return z

    def t(a):
        return np.ascontiguousarray(
            np.asarray(a).T.astype(np.float32, copy=False))

    def vec(a, k):
        return np.ascontiguousarray(
            np.asarray(a).astype(np.float32, copy=False).reshape(k, 128).T)

    conv = np.asarray(iv["L0_attn_conv_w"]).astype(np.float32, copy=False)
    convw = np.ascontiguousarray(
        conv[:, 0, :].reshape(8, 128, 3).transpose(1, 0, 2).reshape(128, 24))
    return {
        "w1t": t(iv["L0_attn_w1_w"]), "v1t": t(iv["L0_attn_v1_w"]),
        "w2t": t(iv["L0_attn_w2_w"]),
        "bmt": t(iv["L0_attn_Bm"]), "cmt": t(iv["L0_attn_Cm"]),
        "amat": np.ascontiguousarray(
            np.asarray(iv["L0_attn_A"]).astype(np.float32, copy=False)),
        "ff1t": t(iv["L0_attn_ff1_w"]), "ff2t": t(iv["L0_attn_ff2_w"]),
        "W1": np.ascontiguousarray(
            np.asarray(iv["L0_ffn_W1"]).astype(np.float32, copy=False)),
        "W2": np.ascontiguousarray(
            np.asarray(iv["L0_ffn_W2"]).astype(np.float32, copy=False)),
        "convw": convw,
        "g1": vec(iv["L0_attn_ln1_g"], 8), "b1ln": vec(iv["L0_attn_ln1_b"], 8),
        "g2": vec(iv["L0_attn_ln2_g"], 8), "b2ln": vec(iv["L0_attn_ln2_b"], 8),
        "w1b": vec(iv["L0_attn_w1_b"], 8), "v1b": vec(iv["L0_attn_v1_b"], 8),
        "w2b": vec(iv["L0_attn_w2_b"], 8),
        "ff1b": vec(iv["L0_attn_ff1_b"], 16),
        "ff2b": vec(iv["L0_attn_ff2_b"], 8),
        "qb1": vec(iv["L0_ffn_b1"], 32), "qb2": vec(iv["L0_ffn_b2"], 8),
        "qth": vec(iv["L0_ffn_quantum_phases"], 32),
        "qwr": vec(iv["L0_ffn_quantum_weights_real"], 32),
        "qwi": vec(iv["L0_ffn_quantum_weights_imag"], 32),
        "qb1p_in": pad256(vec(iv["L0_ffn_b1"], 32)),
        "qthp_in": pad256(vec(iv["L0_ffn_quantum_phases"], 32)),
        "qwrp_in": pad256(vec(iv["L0_ffn_quantum_weights_real"], 32)),
        "qwip_in": pad256(vec(iv["L0_ffn_quantum_weights_imag"], 32)),
    }


def kernel(**inputs):
    NCORES = 8
    x = np.asarray(inputs["L0_attn_x"]).astype(np.float32, copy=False)
    Bt, S, Dm = x.shape
    per = Bt // NCORES
    wmap = _prep_weights(inputs)
    nc, _ = build_program(per, S)
    in_maps = []
    for c in range(NCORES):
        xs = np.ascontiguousarray(
            x[c * per:(c + 1) * per].reshape(per, S, NC_D, 128)
            .transpose(0, 2, 3, 1))
        m = {"x": xs}
        m.update(wmap)
        in_maps.append(m)
    res = run_bass_kernel_spmd(nc, in_maps, core_ids=list(range(NCORES)))
    outs = []
    for c in range(NCORES):
        o = np.asarray(res.results[c]["out"])
        outs.append(o.transpose(0, 3, 1, 2).reshape(per, S, Dm))
    return np.concatenate(outs, axis=0)


# version 09
# speedup vs baseline: 1.0724x; 1.0724x over previous
import math
import numpy as np
import concourse.bass as bass
import concourse.mybir as mybir
from concourse.bass_utils import run_bass_kernel_spmd

F32 = mybir.dt.float32
BF16 = mybir.dt.bfloat16
I32 = mybir.dt.int32
AF = mybir.ActivationFunctionType
ALU = mybir.AluOpType

PI = math.pi
TWO_PI = 2.0 * math.pi
INV_2PI = 1.0 / TWO_PI

D = 1024
NC_D = 8
DFF = 2048
DQ = 4096
DST = 64
T = 512
WD = 514
L = 32
NCHK = T // L
NPAIR = L // 2
QH = 256
LN_EPS = 1e-5


class Buf:
    __slots__ = ("name", "writers", "readers")

    def __init__(self, name):
        self.name = name
        self.writers = {}
        self.readers = {}


class Plan:
    def __init__(self):
        self.streams = {k: [] for k in ("T", "V", "A", "Q", "G")}
        self.cnt = {}
        self.marks = {k: {} for k in ("T", "V", "A", "Q", "G")}

    def step(self, stream, emit, reads=(), writes=(), semkey=None, dma=False):
        semkey = semkey or stream
        need = {}

        def add_need(d):
            for k, v in d.items():
                if k == semkey:
                    continue
                if need.get(k, 0) < v:
                    need[k] = v

        for b in reads:
            add_need(b.writers)
        for b in writes:
            add_need(b.writers)
            add_need(b.readers)
        m = self.marks[stream]
        waits = []
        for k, v in sorted(need.items()):
            if m.get(k, 0) < v:
                m[k] = v
                waits.append((k, v))
        inc = 16 if dma else 1
        self.cnt[semkey] = self.cnt.get(semkey, 0) + inc
        cnt = self.cnt[semkey]
        self.streams[stream].append((waits, emit, semkey, inc))
        for b in reads:
            b.readers[semkey] = cnt
        for b in writes:
            b.writers = {semkey: cnt}
            b.readers = {}
        return cnt


def build_program(n_seq, S, dbg_names=()):
    assert S % T == 0
    tiles_per_seq = S // T
    nc = bass.Bass("TRN2", target_bir_lowering=False,
                   dynamic_dma_scratch_size=16384,
                   detect_race_conditions=False)
    p = Plan()

    def din(name, shape):
        return nc.dram_tensor(name, list(shape), F32, kind="ExternalInput").ap()

    x_d = din("x", (n_seq, NC_D, 128, S))
    w1t_d = din("w1t", (D, D))
    v1t_d = din("v1t", (D, D))
    w2t_d = din("w2t", (D, D))
    bmt_d = din("bmt", (D, DST))
    cmt_d = din("cmt", (DST, D))
    a_d = din("amat", (DST, DST))
    ff1t_d = din("ff1t", (D, DFF))
    ff2t_d = din("ff2t", (DFF, D))
    W1_d = din("W1", (D, DQ))
    W2_d = din("W2", (DQ, D))
    convw_d = din("convw", (128, 24))
    g1_d = din("g1", (128, 8)); b1ln_d = din("b1ln", (128, 8))
    g2_d = din("g2", (128, 8)); b2ln_d = din("b2ln", (128, 8))
    w1b_d = din("w1b", (128, 8)); v1b_d = din("v1b", (128, 8))
    w2b_d = din("w2b", (128, 8)); ff2b_d = din("ff2b", (128, 8))
    ff1b_d = din("ff1b", (128, 16))
    qb1_d = din("qb1", (128, 32)); qb2_d = din("qb2", (128, 8))
    qth_d = din("qth", (128, 32))
    qthp_d = din("qthp_in", (128, 256))
    qwrp_d = din("qwrp_in", (128, 256))
    qwip_d = din("qwip_in", (128, 256))
    qwr_d = din("qwr", (128, 32)); qwi_d = din("qwi", (128, 32))
    qb1p_d = din("qb1p_in", (128, 256))
    out_d = nc.dram_tensor("out", [n_seq, NC_D, 128, S], F32,
                           kind="ExternalOutput").ap()

    W1s_d = nc.dram_tensor("W1s", [32, 128, NC_D, 128], BF16, kind="Internal").ap()
    f1s_d = nc.dram_tensor("f1s", [16, 128, NC_D, 128], BF16, kind="Internal").ap()
    W2s_d = nc.dram_tensor("W2s", [32, 128, D], BF16, kind="Internal").ap()
    f2s_d = nc.dram_tensor("f2s", [16, 128, D], BF16, kind="Internal").ap()

    sb = nc.alloc_sbuf_tensor
    w1t_bf = sb("w1t_bf", [128, NC_D * D], BF16)
    v1t_bf = sb("v1t_bf", [128, NC_D * D], BF16)
    w2t_bf = sb("w2t_bf", [128, NC_D * D], BF16)
    bmt_bf = sb("bmt_bf", [128, NC_D * DST], BF16)
    cmt_bf = sb("cmt_bf", [DST, D], BF16)
    a_f32 = sb("a_f32", [DST, DST], F32)
    a32t = sb("a32t", [DST, DST], F32)
    pbf = sb("pbf", [DST, 33 * DST], BF16)
    qpbf = sb("qpbf", [DST, 17 * DST], BF16)
    ppair = sb("ppair", [128, NPAIR * DST], BF16)
    iota64 = sb("iota64", [DST, DST], I32)
    convw_s = sb("convw_s", [128, 24], F32)
    g1_s = sb("g1_s", [128, 8], F32); b1ln_s = sb("b1ln_s", [128, 8], F32)
    g2_s = sb("g2_s", [128, 8], F32); b2ln_s = sb("b2ln_s", [128, 8], F32)
    w1b_s = sb("w1b_s", [128, 8], F32); v1b_s = sb("v1b_s", [128, 8], F32)
    w2b_s = sb("w2b_s", [128, 8], F32); ff2b_s = sb("ff2b_s", [128, 8], F32)
    ff1b_s = sb("ff1b_s", [128, 16], F32)
    qb1_s = sb("qb1_s", [128, 32], F32); qb2_s = sb("qb2_s", [128, 8], F32)
    qth_s = sb("qth_s", [128, 32], F32)
    qwr_s = sb("qwr_s", [128, 32], F32); qwi_s = sb("qwi_s", [128, 32], F32)
    qthm_s = sb("qthm_s", [128, 256], F32)
    qc_s = sb("qc_s", [128, 256], F32)
    qx_s = sb("qx_s", [128, 128], F32)
    cvec = sb("cvec", [128, 4], F32)
    ones128f = sb("ones128f", [1, 128], F32)
    oneskbf = sb("oneskbf", [128, 1], BF16)
    rows = sb("rows", [1, 2056], F32)
    xa = sb("xa", [128, NC_D * WD], F32)
    xnbf = sb("xnbf", [128, NC_D * WD], BF16)
    h1bf = sb("h1bf", [128, NC_D * WD], BF16)
    x1r = sb("x1r", [128, NC_D * WD], F32)
    xcbf = sb("xcbf", [128, NC_D * T], BF16)
    mid = sb("mid", [128, NC_D * T], F32)
    ffn = sb("ffn", [128, NC_D * T], F32)
    pf32 = nc.alloc_sbuf_tensor_at("pf32", [DST, 33 * DST], F32,
                                   offset=nc.lookup_mloc(mid).addr)
    qp32 = nc.alloc_sbuf_tensor_at("qp32", [DST, 17 * DST], F32,
                                   offset=nc.lookup_mloc(mid).addr + 8448)
    _mo = nc.lookup_mloc(mid).addr
    qthp = nc.alloc_sbuf_tensor_at("qthp", [128, 256], F32, offset=_mo + 12800)
    qwrp = nc.alloc_sbuf_tensor_at("qwrp", [128, 256], F32, offset=_mo + 13824)
    qwip = nc.alloc_sbuf_tensor_at("qwip", [128, 256], F32, offset=_mo + 14848)
    upad = sb("upad", [128, 1024], BF16)
    hs = sb("hs", [DST, NCHK + 1], BF16)
    esb = sb("esb", [DST, NCHK], BF16)
    hbf = sb("hbf", [DST, T], BF16)
    hidr = sb("hidr", [128, 3 * T], F32)
    mr = sb("mr", [128, 3 * T], F32)
    kr = sb("kr", [128, T], F32)
    MAGIC = float(1.5 * 2 ** 23)
    rw1 = sb("rw1", [128, 3 * NC_D * 128], BF16)
    rw2 = sb("rw2", [128, 6 * D], BF16)
    rf1 = sb("rf1", [128, 2 * NC_D * 128], BF16)
    rf2 = sb("rf2", [128, 2 * D], BF16)
    gelbf = nc.alloc_sbuf_tensor_at("gelbf", [128, 16 * T], BF16,
                                    offset=nc.lookup_mloc(x1r).addr)
    rbf = nc.alloc_sbuf_tensor_at("rbf", [128, 6 * T], BF16,
                                  offset=nc.lookup_mloc(xcbf).addr)

    SM = nc.alloc_psum_tensor("SM", [128, 1024], F32)
    BCA = nc.alloc_psum_tensor("BCA", [128, 1024], F32)
    BCB = nc.alloc_psum_tensor("BCB", [128, 1024], F32)
    MMX = nc.alloc_psum_tensor("MMX", [128, 1024], F32)

    B = {}
    for name in ["CONST", "W1TBF", "V1TBF", "W2TBF", "BMTBF", "CMTBF",
                 "AF32", "PF32", "PBF", "PPAIR", "IOTA",
                 "XA", "XNBF", "H1BF", "X1R", "XCBF", "MID",
                 "UPAD", "HS", "ESB", "HBF", "ROWS",
                 "SM0", "SM1", "BCA0", "BCA1", "BCB0", "BCB1", "MMX0", "MMX1",
                 "QTHM", "QC", "QX", "QPAD", "RES", "SCRALL"]:
        B[name] = Buf(name)
    FFNq = [Buf(f"FFN{i}") for i in range(4)]
    HIDq = [Buf(f"HID{i}") for i in range(3)]
    MRq = [Buf(f"MR{i}") for i in range(3)]
    KRq = [Buf(f"KR{i}") for i in range(2)]
    RBFq = [Buf(f"RBF{i}") for i in range(6)]
    RW1q = [Buf(f"RW1{i}") for i in range(3)]
    RW2q = [Buf(f"RW2{i}") for i in range(6)]
    RF1q = [Buf(f"RF1{i}") for i in range(4)]
    RF2q = [Buf(f"RF2{i}") for i in range(4)]
    SCRSLOT = [Buf(f"SCRS{i}") for i in range(4)]
    ACCB = (B["BCA0"], B["BCA0"], B["BCA1"], B["BCA1"],
            B["BCB0"], B["BCB0"], B["BCB1"], B["BCB1"])

    st = p.step

    def vstep(emit, reads=(), writes=()):
        st("V", emit, reads=reads, writes=writes)

    def astep(emit, reads=(), writes=()):
        st("A", emit, reads=reads, writes=writes)

    def tstep(emit, reads=(), writes=()):
        st("T", emit, reads=reads, writes=writes)

    def gdma(emit, reads=(), writes=(), semkey="GO"):
        st("G", emit, reads=reads, writes=writes, semkey=semkey, dma=True)

    def vv(*fns):
        def emit(e):
            last = None
            for f in fns:
                last = f(e)
            return last
        return emit

    def qload(dst_ap, src_ap, wbuf):
        st("Q", lambda e, d=dst_ap, s=src_ap: e.dma_start(d, s),
           writes=(B["CONST"],), semkey="QC", dma=True)

    qload(convw_s[:], convw_d, B["CONST"])
    for dst, src in [(g1_s, g1_d), (b1ln_s, b1ln_d), (g2_s, g2_d),
                     (b2ln_s, b2ln_d), (w1b_s, w1b_d), (v1b_s, v1b_d),
                     (w2b_s, w2b_d), (ff2b_s, ff2b_d), (ff1b_s, ff1b_d),
                     (qb1_s, qb1_d), (qb2_s, qb2_d), (qth_s, qth_d),
                     (qwr_s, qwr_d), (qwi_s, qwi_d)]:
        qload(dst[:], src, B["CONST"])
    qload(qthp[:], qthp_d, B["CONST"])
    qload(qwrp[:], qwrp_d, B["CONST"])
    qload(qwip[:], qwip_d, B["CONST"])
    qload(a_f32[:], a_d, B["CONST"])

    vstep(vv(lambda e: e.memset(cvec[:, 0:1], LN_EPS),
             lambda e: e.memset(cvec[:, 1:2], 0.1),
             lambda e: e.memset(cvec[:, 2:3], PI / 2),
             lambda e: e.memset(ones128f[:], 1.0),
             lambda e: e.memset(oneskbf[:], 1.0 / 1024.0),
             lambda e: e.memset(upad[:], 0.0)),
          writes=(B["CONST"], B["UPAD"]))

    def cast_res(dst_ap, src_ap, wbuf):
        st("G", lambda e, d=dst_ap, s=src_ap: e.dma_start(d, s),
           writes=(B["RES"],), semkey="GR", dma=True)

    cast_res(w1t_bf[:].rearrange("p (k d) -> p k d", k=NC_D),
             w1t_d.rearrange("(k p) d -> p k d", p=128), B["RES"])
    cast_res(v1t_bf[:].rearrange("p (k d) -> p k d", k=NC_D),
             v1t_d.rearrange("(k p) d -> p k d", p=128), B["RES"])
    cast_res(w2t_bf[:].rearrange("p (k d) -> p k d", k=NC_D),
             w2t_d.rearrange("(k p) d -> p k d", p=128), B["RES"])
    cast_res(bmt_bf[:].rearrange("p (k s) -> p k s", k=NC_D),
             bmt_d.rearrange("(k p) s -> p k s", p=128), B["RES"])
    cast_res(cmt_bf[:], cmt_d, B["RES"])

    st("G", lambda e: e.iota(iota64[:], pattern=[[1, DST]], base=0,
                             channel_multiplier=-1),
       writes=(B["IOTA"],), semkey="GC")
    vstep(vv(lambda e: e.tensor_scalar(pf32[:, 0:DST], iota64[:], 0.0, None,
                                       ALU.is_equal),
             lambda e: e.tensor_scalar(pbf[:, 0:DST], iota64[:], 0.0, None,
                                       ALU.is_equal)),
          reads=(B["IOTA"],), writes=(B["PF32"], B["PBF"]))

    for j in range(1, 33):
        def emit_mm(e, j=j):
            return e.matmul(SM[0:DST, 0:DST], a_f32[:],
                            pf32[:, (j - 1) * DST:j * DST],
                            start=True, stop=True)
        tstep(emit_mm, reads=(B["CONST"], B["PF32"]), writes=(B["SM0"],))
        def emit_cp(e, j=j):
            e.tensor_copy(pf32[:, j * DST:(j + 1) * DST], SM[0:DST, 0:DST])
            return e.tensor_copy(pbf[:, j * DST:(j + 1) * DST],
                                 SM[0:DST, 0:DST])
        vstep(emit_cp, reads=(B["SM0"],), writes=(B["PF32"], B["PBF"]))
    for jj in range(NPAIR):
        def emit_mm(e, jj=jj):
            return e.matmul(SM[64:128, 0:DST], a_f32[:],
                            pf32[:, (2 * jj) * DST:(2 * jj + 1) * DST],
                            start=True, stop=True)
        tstep(emit_mm, reads=(B["CONST"], B["PF32"]), writes=(B["SM0"],))
        def emit_cp(e, jj=jj):
            e.tensor_copy(ppair[0:64, jj * DST:(jj + 1) * DST],
                          pbf[:, (2 * jj) * DST:(2 * jj + 1) * DST])
            return e.tensor_copy(ppair[64:128, jj * DST:(jj + 1) * DST],
                                 SM[64:128, 0:DST])
        vstep(emit_cp, reads=(B["SM0"], B["PBF"]), writes=(B["PPAIR"],))

    def emit_tr(e):
        return e.transpose(SM[0:DST, 0:DST], pf32[:, 32 * DST:33 * DST],
                           pf32[:, 0:DST])
    tstep(emit_tr, reads=(B["PF32"],), writes=(B["SM0"],))
    vstep(lambda e: e.tensor_copy(a32t[:], SM[0:DST, 0:DST]),
          reads=(B["SM0"],), writes=(B["PF32"],))
    vstep(vv(lambda e: e.tensor_copy(qp32[:, 0:DST], pf32[:, 0:DST]),
             lambda e: e.tensor_copy(qpbf[:, 0:DST], pf32[:, 0:DST])),
          reads=(B["PF32"],), writes=(B["PBF"],))
    for k in range(1, 17):
        def emit_qmm(e, k=k):
            return e.matmul(SM[0:DST, 0:DST], a32t[:],
                            qp32[:, (k - 1) * DST:k * DST],
                            start=True, stop=True)
        tstep(emit_qmm, reads=(B["PF32"], B["PBF"]), writes=(B["SM0"],))
        def emit_qcp(e, k=k):
            e.tensor_copy(qp32[:, k * DST:(k + 1) * DST], SM[0:DST, 0:DST])
            return e.tensor_copy(qpbf[:, k * DST:(k + 1) * DST],
                                 SM[0:DST, 0:DST])
        vstep(emit_qcp, reads=(B["SM0"],), writes=(B["PBF"],))

    QP = 256
    k0 = kr[:, 0:QP]
    k1 = kr[:, QP:2 * QP]
    m0 = mr[:, 0:QP]
    m1 = mr[:, QP:2 * QP]
    h0 = hidr[:, 0:QP]
    vstep(vv(lambda e: e.tensor_scalar(k0, qthp[:], INV_2PI, MAGIC,
                                       ALU.mult, ALU.add),
             lambda e: e.tensor_scalar(k1, k0, MAGIC, 0.0,
                                       ALU.subtract, ALU.add),
             lambda e: e.scalar_tensor_tensor(qthm_s[:], k1, -TWO_PI,
                                              qthp[:], ALU.mult, ALU.add),
             lambda e: e.tensor_scalar(m0, qthm_s[:], PI / 2, 0.0,
                                       ALU.add, ALU.add),
             lambda e: e.tensor_scalar(k0, m0, INV_2PI, MAGIC,
                                       ALU.mult, ALU.add),
             lambda e: e.tensor_scalar(k1, k0, MAGIC, 0.0,
                                       ALU.subtract, ALU.add),
             lambda e: e.scalar_tensor_tensor(m1, k1, -TWO_PI, m0,
                                              ALU.mult, ALU.add)),
          reads=(B["CONST"],),
          writes=(B["QTHM"], KRq[0], MRq[0]))
    astep(vv(lambda e: e.activation(h0, qthm_s[:], AF.Sin),
             lambda e: e.activation(m0, m1, AF.Sin)),
          reads=(B["QTHM"], MRq[0]), writes=(HIDq[0], MRq[0]))
    qload(qthp[:], qb1p_d, B["CONST"])
    vstep(vv(lambda e: e.tensor_tensor(k0, m0, qwrp[:], ALU.mult),
             lambda e: e.tensor_tensor(k1, h0, qwip[:], ALU.mult),
             lambda e: e.tensor_tensor(k0, k0, k1, ALU.add),
             lambda e: e.tensor_tensor(k0, k0, qwip[:], ALU.mult),
             lambda e: e.tensor_scalar(qc_s[:], k0, 0.1, 0.0,
                                       ALU.mult, ALU.add),
             lambda e: e.tensor_tensor(k1, qc_s[:], h0, ALU.mult),
             lambda e: e.tensor_tensor(k0, qc_s[:], m0, ALU.mult),
             lambda e: e.tensor_scalar(m1, qthp[:], 0.1, 0.0,
                                       ALU.mult, ALU.add),
             lambda e: e.tensor_scalar(h0, m1, cvec[:, 2:3], 0.0,
                                       ALU.add, ALU.add)),
          reads=(B["CONST"], B["QTHM"], HIDq[0], MRq[0], KRq[0]),
          writes=(B["QC"], MRq[0], HIDq[0], KRq[0]))
    vstep(vv(lambda e: e.tensor_copy(qx_s[:, 0:32], mr[:, QP:QP + 32]),
             lambda e: e.tensor_copy(qx_s[:, 32:64], hidr[:, 0:32]),
             lambda e: e.tensor_copy(qx_s[:, 64:96], kr[:, QP:QP + 32]),
             lambda e: e.tensor_copy(qx_s[:, 96:128], kr[:, 0:32])),
          reads=(MRq[0], HIDq[0], KRq[0]),
          writes=(B["QX"],))
    if "boot" in dbg_names:
        bq = nc.dram_tensor("dbg_bqthm", [128, 32], F32, kind="ExternalOutput").ap()
        bs = nc.dram_tensor("dbg_bsth", [128, 32], F32, kind="ExternalOutput").ap()
        bc = nc.dram_tensor("dbg_bqc", [128, 32], F32, kind="ExternalOutput").ap()
        gdma(lambda e: e.dma_start(bq, qthm_s[:, 0:32]), reads=(B["QTHM"],))
        gdma(lambda e: e.dma_start(bs, hidr[:, 0:32]), reads=(HIDq[0],))
        gdma(lambda e: e.dma_start(bc, qc_s[:, 0:32]), reads=(B["QC"],))

    def emit_xload(seq, ti):
        t0 = ti * T
        first = (ti == 0)
        last_t = (ti == tiles_per_seq - 1)
        xsrc = x_d[seq].rearrange("c p t -> p c t")
        xa3 = xa[:].rearrange("p (c w) -> p c w", w=WD)
        if first and last_t:
            gdma(lambda e, d=xa3[:, :, 1:1 + S], s=xsrc[:, :, 0:S]:
                 e.dma_start(d, s), writes=(B["XA"],), semkey="GX")
            vstep(vv(lambda e: e.memset(xa3[:, :, 0:1], 0.0),
                     lambda e: e.memset(xa3[:, :, WD - 1:WD], 0.0)),
                  writes=(B["XA"],))
        elif first:
            gdma(lambda e, d=xa3[:, :, 1:WD], s=xsrc[:, :, 0:T + 1]:
                 e.dma_start(d, s), writes=(B["XA"],), semkey="GX")
            vstep(lambda e: e.memset(xa3[:, :, 0:1], 0.0),
                  writes=(B["XA"],))
        elif last_t:
            gdma(lambda e, d=xa3[:, :, 0:WD - 1],
                 s=xsrc[:, :, t0 - 1:t0 + T]: e.dma_start(d, s),
                 writes=(B["XA"],), semkey="GX")
            vstep(lambda e: e.memset(xa3[:, :, WD - 1:WD], 0.0),
                  writes=(B["XA"],))
        else:
            gdma(lambda e, d=xa3, s=xsrc[:, :, t0 - 1:t0 + T + 1]:
                 e.dma_start(d, s), writes=(B["XA"],), semkey="GX")

    emit_xload(0, 0)
    all_tiles = [(s2, t2) for s2 in range(n_seq) for t2 in range(tiles_per_seq)]

    scr_prio = ([("F1", m) for m in range(16)] +
                [("F2", q) for q in range(16)] +
                [("W1", m) for m in range(3)] +
                [("W2", q) for q in range(6)])
    scr_rest = ([("W1", m) for m in range(3, 32)] +
                [("W2", q) for q in range(6, 32)])
    scrb = {key: Buf("SCR" + key[0] + str(key[1]))
            for key in scr_prio + scr_rest}
    _scri = [0]

    def emit_scr(chunks):
        for kind, idx in chunks:
            i = _scri[0]
            _scri[0] += 1
            slot = i % 4
            stage = ffn[:, slot * 1024:(slot + 1) * 1024]
            stage3 = stage.rearrange("p (k m) -> p k m", k=NC_D)
            if kind == "W1":
                src = W1_d[:, idx * 128:(idx + 1) * 128].rearrange(
                    "(k p) m -> p k m", p=128)
                dst = W1s_d[idx]
                ssrc, sdst = stage3, stage3
            elif kind == "F1":
                src = ff1t_d[:, idx * 128:(idx + 1) * 128].rearrange(
                    "(k p) m -> p k m", p=128)
                dst = f1s_d[idx]
                ssrc, sdst = stage3, stage3
            elif kind == "W2":
                src = W2_d[idx * 128:(idx + 1) * 128, :]
                dst = W2s_d[idx]
                ssrc, sdst = stage, stage
            else:
                src = ff2t_d[idx * 128:(idx + 1) * 128, :]
                dst = f2s_d[idx]
                ssrc, sdst = stage, stage
            st("Q", lambda e, d=sdst, s=src: e.dma_start(d, s),
               writes=(FFNq[slot],), semkey=f"QS{slot}", dma=True)
            st("G", lambda e, d=dst, s=ssrc: e.dma_start(d, s),
               reads=(FFNq[slot],),
               writes=(SCRSLOT[slot], scrb[(kind, idx)]),
               semkey=f"GS{slot}", dma=True)

    emit_scr(scr_prio)

    class Ring:
        def __init__(self, name, tensor, slot_w, bufs, seq_fn, nuses,
                     nslots=4, dep_fn=None):
            self.name = name
            self.tensor = tensor
            self.slot_w = slot_w
            self.bufs = bufs
            self.seq_fn = seq_fn
            self.nuses = nuses
            self.nslots = nslots
            self.dep_fn = dep_fn
            self.filled = 0
            self.used = 0

        def fill_next(self):
            if self.filled >= self.nuses:
                return
            i = self.filled
            self.filled += 1
            slot = i % self.nslots
            dst = self.tensor[:, slot * self.slot_w:(slot + 1) * self.slot_w]
            src = self.seq_fn(i)
            st("Q", lambda e, d=dst, s=src: e.dma_start(d, s),
               reads=(self.dep_fn(i),), writes=(self.bufs[slot],),
               semkey=self.name + str(slot), dma=True)

        def use(self):
            i = self.used
            self.used += 1
            slot = i % self.nslots
            return (self.tensor[:, slot * self.slot_w:(slot + 1) * self.slot_w],
                    self.bufs[slot])

    total_tiles = n_seq * tiles_per_seq
    ring_w1 = Ring("qw1", rw1, NC_D * 128, RW1q,
                   lambda i: W1s_d[i % 32], total_tiles * 32, nslots=3,
                   dep_fn=lambda i: scrb[("W1", i % 32)])
    ring_w2 = Ring("qw2", rw2, D, RW2q,
                   lambda i: W2s_d[i % 32], total_tiles * 32, nslots=6,
                   dep_fn=lambda i: scrb[("W2", i % 32)])
    ring_f1 = Ring("qf1", rf1, NC_D * 128, RF1q,
                   lambda i: f1s_d[i % 16], total_tiles * 16, nslots=2,
                   dep_fn=lambda i: scrb[("F1", i % 16)])
    ring_f2 = Ring("qf2", rf2, D, RF2q,
                   lambda i: f2s_d[i % 16], total_tiles * 16, nslots=2,
                   dep_fn=lambda i: scrb[("F2", i % 16)])
    for r in (ring_w1, ring_w2, ring_f1, ring_f2):
        for _ in range(r.nslots):
            r.fill_next()
    emit_scr(scr_rest)

    def emit_ln(x_ap, xbuf, out_f32_ap, outbuf, width, gb=None,
                out_bf_ap=None, outbf_buf=None, extra_cast=None,
                skip_cast=False, cast_t=None, cast_buf=None, cast_stride=None,
                sq_t=None, sq_buf=None, sq_stride=None):
        stride = WD if width == WD else T
        halo = (width == WD)
        ct = xnbf if cast_t is None else cast_t
        cs = WD if cast_stride is None else cast_stride
        castbuf = B["XNBF"] if cast_buf is None else cast_buf
        sqt = h1bf if sq_t is None else sq_t
        sqs = WD if sq_stride is None else sq_stride
        sqbuf = B["H1BF"] if sq_buf is None else sq_buf
        if not skip_cast:
            def emit_cast(e):
                last = None
                for c in range(NC_D):
                    last = e.activation(
                        ct[:, c * cs:c * cs + width],
                        x_ap[:, c * stride:c * stride + width], AF.Identity)
                return last
            astep(emit_cast, reads=(xbuf,), writes=(castbuf,))

        def emit_sq(e):
            last = None
            for c in range(NC_D):
                last = e.activation(sqt[:, c * sqs:c * sqs + width],
                                    ct[:, c * cs:c * cs + width], AF.Square)
            return last
        astep(emit_sq, reads=(castbuf,), writes=(sqbuf,))

        def emit_mu(e):
            for k in range(NC_D):
                e.matmul(SM[0:1, 0:512], oneskbf[:],
                         ct[:, k * cs:k * cs + 512],
                         start=(k == 0), stop=(k == NC_D - 1))
            last = None
            for k in range(NC_D):
                last = e.matmul(SM[0:1, 512:1024], oneskbf[:],
                                sqt[:, k * sqs:k * sqs + 512],
                                start=(k == 0), stop=(k == NC_D - 1))
            return last
        tstep(emit_mu, reads=(castbuf, sqbuf, B["CONST"]),
              writes=(B["SM0"], B["SM1"]))
        if halo:
            def emit_halo(e):
                for k in range(NC_D):
                    e.matmul(BCB[0:1, 516:518], oneskbf[:],
                             ct[:, k * cs + 512:k * cs + 514],
                             start=(k == 0), stop=(k == NC_D - 1))
                last = None
                for k in range(NC_D):
                    last = e.matmul(BCB[0:1, 518:520], oneskbf[:],
                                    sqt[:, k * sqs + 512:k * sqs + 514],
                                    start=(k == 0), stop=(k == NC_D - 1))
                return last
            tstep(emit_halo, reads=(castbuf, sqbuf, B["CONST"]),
                  writes=(B["BCB1"],))

        def emit_rows1(e):
            e.tensor_copy(rows[0:1, 0:512], SM[0:1, 0:512])
            last = e.tensor_copy(rows[0:1, 1028:1540], SM[0:1, 512:1024])
            if halo:
                e.tensor_copy(rows[0:1, 512:514], BCB[0:1, 516:518])
                last = e.tensor_copy(rows[0:1, 1540:1542], BCB[0:1, 518:520])
            return last
        vstep(emit_rows1,
              reads=(B["SM0"], B["SM1"]) + ((B["BCB1"],) if halo else ()),
              writes=(B["ROWS"],))

        def emit_rows2(e):
            e.tensor_tensor(rows[0:1, 1542:1542 + width], rows[0:1, 0:width],
                            rows[0:1, 0:width], ALU.mult)
            return e.tensor_tensor(rows[0:1, 1028:1028 + width],
                                   rows[0:1, 1028:1028 + width],
                                   rows[0:1, 1542:1542 + width], ALU.subtract)
        vstep(emit_rows2, reads=(B["ROWS"],), writes=(B["ROWS"],))
        astep(lambda e: e.activation(rows[0:1, 1542:1542 + width],
                                     rows[0:1, 1028:1028 + width], AF.Sqrt,
                                     bias=cvec[0:1, 0:1]),
              reads=(B["ROWS"], B["CONST"]), writes=(B["ROWS"],))
        vstep(lambda e: e.reciprocal(rows[0:1, 514:514 + width],
                                     rows[0:1, 1542:1542 + width]),
              reads=(B["ROWS"],), writes=(B["ROWS"],))

        def emit_bc(e):
            e.matmul(SM[:, 0:512], ones128f[:], rows[0:1, 0:512],
                     start=True, stop=True)
            last = e.matmul(BCB[:, 0:512], ones128f[:], rows[0:1, 514:1026],
                            start=True, stop=True)
            if halo:
                e.matmul(SM[:, 512:514], ones128f[:], rows[0:1, 512:514],
                         start=True, stop=True)
                last = e.matmul(BCB[:, 512:514], ones128f[:],
                                rows[0:1, 1026:1028], start=True, stop=True)
            return last
        tstep(emit_bc, reads=(B["ROWS"], B["CONST"]),
              writes=(B["SM0"], B["BCB0"]) +
                     ((B["SM1"], B["BCB1"]) if halo else ()))

        rb = (B["SM0"], B["BCB0"]) + ((B["SM1"], B["BCB1"]) if halo else ())

        def emit_ap1(e):
            last = None
            for c in range(NC_D):
                last = e.tensor_tensor(
                    out_f32_ap[:, c * stride:c * stride + width],
                    x_ap[:, c * stride:c * stride + width],
                    SM[:, 0:width], ALU.subtract)
            return last
        vstep(emit_ap1, reads=(xbuf,) + rb, writes=(outbuf,))

        def emit_ap2(e):
            last = None
            for c in range(NC_D):
                last = e.tensor_tensor(
                    out_f32_ap[:, c * stride:c * stride + width],
                    out_f32_ap[:, c * stride:c * stride + width],
                    BCB[:, 0:width], ALU.mult)
            return last
        vstep(emit_ap2, reads=rb + (outbuf,), writes=(outbuf,))
        if gb is not None:
            g_ap, b_ap = gb
            def emit_gb(e):
                last = None
                for c in range(NC_D):
                    last = e.tensor_scalar(
                        out_bf_ap[:, c * WD:c * WD + width],
                        out_f32_ap[:, c * stride:c * stride + width],
                        g_ap[:, c:c + 1], b_ap[:, c:c + 1],
                        ALU.mult, ALU.add)
                return last
            vstep(emit_gb, reads=(outbuf, B["CONST"]), writes=(outbf_buf,))
        if extra_cast is not None:
            dst_bf_ap, dstbuf = extra_cast
            def emit_ec(e):
                last = None
                for c in range(NC_D):
                    last = e.activation(
                        dst_bf_ap[:, c * WD:c * WD + width],
                        out_f32_ap[:, c * stride:c * stride + width],
                        AF.Identity)
                return last
            astep(emit_ec, reads=(outbuf,), writes=(dstbuf,))

    def ln_stage_list(seq, ti):
        stages = []

        def emit_cast0(e):
            last = None
            for c in range(NC_D):
                last = e.activation(h1bf[:, c * WD:c * WD + WD],
                                    xa[:, c * WD:c * WD + WD], AF.Identity)
            return last
        stages.append(lambda: astep(emit_cast0, reads=(B["XA"],),
                                    writes=(B["H1BF"],)))

        def emit_sq0(e):
            last = None
            for c in range(NC_D):
                last = e.activation(gelbf[:, c * WD:c * WD + WD],
                                    h1bf[:, c * WD:c * WD + WD], AF.Square)
            return last
        stages.append(lambda: astep(emit_sq0, reads=(B["H1BF"],),
                                    writes=(B["X1R"],)))

        def emit_mu0(e):
            for k in range(NC_D):
                e.matmul(SM[0:1, 0:512], oneskbf[:],
                         h1bf[:, k * WD:k * WD + 512],
                         start=(k == 0), stop=(k == NC_D - 1))
            last = None
            for k in range(NC_D):
                last = e.matmul(SM[0:1, 512:1024], oneskbf[:],
                                gelbf[:, k * WD:k * WD + 512],
                                start=(k == 0), stop=(k == NC_D - 1))
            return last
        stages.append(lambda: tstep(emit_mu0,
                                    reads=(B["H1BF"], B["X1R"], B["CONST"]),
                                    writes=(B["SM0"], B["SM1"])))

        def emit_halo0(e):
            for k in range(NC_D):
                e.matmul(BCB[0:1, 516:518], oneskbf[:],
                         h1bf[:, k * WD + 512:k * WD + 514],
                         start=(k == 0), stop=(k == NC_D - 1))
            last = None
            for k in range(NC_D):
                last = e.matmul(BCB[0:1, 518:520], oneskbf[:],
                                gelbf[:, k * WD + 512:k * WD + 514],
                                start=(k == 0), stop=(k == NC_D - 1))
            return last
        stages.append(lambda: tstep(emit_halo0,
                                    reads=(B["H1BF"], B["X1R"], B["CONST"]),
                                    writes=(B["BCB1"],)))

        def emit_rows0(e):
            e.tensor_copy(rows[0:1, 0:512], SM[0:1, 0:512])
            e.tensor_copy(rows[0:1, 1028:1540], SM[0:1, 512:1024])
            e.tensor_copy(rows[0:1, 512:514], BCB[0:1, 516:518])
            e.tensor_copy(rows[0:1, 1540:1542], BCB[0:1, 518:520])
            e.tensor_tensor(rows[0:1, 1542:2056], rows[0:1, 0:514],
                            rows[0:1, 0:514], ALU.mult)
            return e.tensor_tensor(rows[0:1, 1028:1542],
                                   rows[0:1, 1028:1542],
                                   rows[0:1, 1542:2056], ALU.subtract)
        stages.append(lambda: vstep(emit_rows0,
                                    reads=(B["SM0"], B["SM1"], B["BCB1"]),
                                    writes=(B["ROWS"],)))

        stages.append(lambda: astep(
            lambda e: e.activation(rows[0:1, 1542:2056],
                                   rows[0:1, 1028:1542], AF.Sqrt,
                                   bias=cvec[0:1, 0:1]),
            reads=(B["ROWS"], B["CONST"]), writes=(B["ROWS"],)))
        stages.append(lambda: vstep(
            lambda e: e.reciprocal(rows[0:1, 514:1028],
                                   rows[0:1, 1542:2056]),
            reads=(B["ROWS"],), writes=(B["ROWS"],)))

        def emit_r2a(e):
            e.tensor_tensor(rows[0:1, 1542:2056], rows[0:1, 1028:1542],
                            rows[0:1, 514:1028], ALU.mult)
            return e.tensor_tensor(rows[0:1, 1542:2056],
                                   rows[0:1, 1542:2056],
                                   rows[0:1, 514:1028], ALU.mult)
        stages.append(lambda: vstep(emit_r2a, reads=(B["ROWS"],),
                                    writes=(B["ROWS"],)))
        stages.append(lambda: astep(
            lambda e: e.activation(rows[0:1, 1542:2056],
                                   rows[0:1, 1542:2056], AF.Sqrt,
                                   bias=cvec[0:1, 0:1]),
            reads=(B["ROWS"], B["CONST"]), writes=(B["ROWS"],)))

        def emit_r2b(e):
            e.reciprocal(rows[0:1, 1028:1542], rows[0:1, 1542:2056])
            return e.tensor_tensor(rows[0:1, 1028:1542],
                                   rows[0:1, 1028:1542],
                                   rows[0:1, 514:1028], ALU.mult)
        stages.append(lambda: vstep(emit_r2b, reads=(B["ROWS"],),
                                    writes=(B["ROWS"],)))

        def emit_bc0(e):
            e.matmul(SM[:, 0:512], ones128f[:], rows[0:1, 0:512],
                     start=True, stop=True)
            e.matmul(SM[:, 512:514], ones128f[:], rows[0:1, 512:514],
                     start=True, stop=True)
            e.matmul(BCB[:, 0:512], ones128f[:], rows[0:1, 1028:1540],
                     start=True, stop=True)
            return e.matmul(BCB[:, 512:514], ones128f[:],
                            rows[0:1, 1540:1542], start=True, stop=True)
        stages.append(lambda: tstep(emit_bc0, reads=(B["ROWS"], B["CONST"]),
                                    writes=(B["SM0"], B["SM1"],
                                            B["BCB0"], B["BCB1"])))

        def emit_ap1(e):
            last = None
            for c in range(NC_D):
                last = e.tensor_tensor(x1r[:, c * WD:c * WD + WD],
                                       xa[:, c * WD:c * WD + WD],
                                       SM[:, 0:514], ALU.subtract)
            return last
        stages.append(lambda: vstep(emit_ap1,
                                    reads=(B["XA"], B["SM0"], B["SM1"]),
                                    writes=(B["X1R"],)))

        def emit_l1a(e):
            last = None
            for c in range(NC_D):
                last = e.tensor_tensor(x1r[:, c * WD:c * WD + WD],
                                       x1r[:, c * WD:c * WD + WD],
                                       BCB[:, 0:514], ALU.mult)
            return last
        stages.append(lambda: vstep(emit_l1a,
                                    reads=(B["X1R"], B["BCB0"], B["BCB1"]),
                                    writes=(B["X1R"],)))

        def emit_l1b(e):
            last = None
            for c in range(NC_D):
                last = e.tensor_scalar(h1bf[:, c * WD:c * WD + WD],
                                       x1r[:, c * WD:c * WD + WD],
                                       g1_s[:, c:c + 1],
                                       b1ln_s[:, c:c + 1],
                                       ALU.mult, ALU.add)
            return last
        stages.append(lambda: vstep(emit_l1b, reads=(B["X1R"], B["CONST"]),
                                    writes=(B["H1BF"],)))

        def emit_bc1(e):
            e.matmul(SM[:, 0:512], ones128f[:], rows[0:1, 1542:2054],
                     start=True, stop=True)
            return e.matmul(SM[:, 512:514], ones128f[:],
                            rows[0:1, 2054:2056], start=True, stop=True)
        stages.append(lambda: tstep(emit_bc1, reads=(B["ROWS"], B["CONST"]),
                                    writes=(B["SM0"], B["SM1"])))

        def emit_ap2(e):
            last = None
            for c in range(NC_D):
                last = e.tensor_tensor(xnbf[:, c * WD:c * WD + WD],
                                       x1r[:, c * WD:c * WD + WD],
                                       SM[:, 0:514], ALU.mult)
            return last
        stages.append(lambda: vstep(emit_ap2,
                                    reads=(B["X1R"], B["SM0"], B["SM1"]),
                                    writes=(B["XNBF"],)))
        return stages

    dbg_specs = []
    dbg_done = [False]


    for tile_idx, (seq, ti) in enumerate(all_tiles):
        if True:
            t0 = ti * T
            first = (ti == 0)
            last_t = (ti == tiles_per_seq - 1)

            if tile_idx == 0:
                fdst = ffn[:].rearrange("p (c w) -> p c w", w=T)
                fsrc = x_d[0].rearrange("c p t -> p c t")[:, :, 0:T]
                gdma(lambda e, d=fdst, s=fsrc: e.dma_start(d, s),
                     writes=tuple(FFNq), semkey="GF")
            else:
                def emit_init(e):
                    last = None
                    for c in range(NC_D):
                        last = e.tensor_scalar(ffn[:, c * T:(c + 1) * T],
                                               xa[:, c * WD + 1:c * WD + 513],
                                               qb2_s[:, c:c + 1], None,
                                               ALU.add)
                    return last
                st("V", emit_init, reads=(B["XA"], B["CONST"]),
                   writes=tuple(FFNq))

            if tile_idx == 0:
                for _s in ln_stage_list(seq, ti):
                    _s()

            def phase_dump(tag, tsr, bb, shp, dt):
                if seq == 0 and ti == 0 and ("p_" + tag) in dbg_names:
                    dd = nc.dram_tensor("dbg_p_" + tag, shp, dt,
                                        kind="ExternalOutput").ap()
                    gdma(lambda e, d=dd, s=tsr: e.dma_start(d, s[:]),
                         reads=(bb,))
            phase_dump("h1", h1bf, B["H1BF"], [128, NC_D * WD], BF16)
            phase_dump("w1t", w1t_bf, B["RES"], [128, NC_D * D], BF16)
            phase_dump("v1t", v1t_bf, B["RES"], [128, NC_D * D], BF16)

            for c in range(NC_D):
                rot = MMX if (c % 2 == 0) else BCA
                rb0 = B["MMX0"] if (c % 2 == 0) else B["BCA0"]
                rb1 = B["MMX1"] if (c % 2 == 0) else B["BCA1"]
                def emit_mm(e, c=c, rot=rot):
                    for k in range(NC_D):
                        e.matmul(rot[:, 0:512],
                                 w1t_bf[:, k * D + c * 128:k * D + c * 128 + 128],
                                 h1bf[:, k * WD:k * WD + 512],
                                 start=(k == 0), stop=(k == NC_D - 1))
                    last = None
                    for k in range(NC_D):
                        last = e.matmul(rot[:, 512:514],
                                 w1t_bf[:, k * D + c * 128:k * D + c * 128 + 128],
                                 h1bf[:, k * WD + 512:k * WD + 514],
                                 start=(k == 0), stop=(k == NC_D - 1))
                    return last
                tstep(emit_mm, reads=(B["H1BF"], B["RES"]), writes=(rb0, rb1))
                def emit_ev(e, c=c, rot=rot):
                    return e.activation(x1r[:, c * WD:c * WD + WD],
                                        rot[:, 0:514], AF.Identity,
                                        bias=w1b_s[:, c:c + 1])
                astep(emit_ev, reads=(rb0, rb1, B["CONST"]),
                      writes=(B["X1R"],))

            x13 = x1r[:].rearrange("p (c w) -> p c w", w=WD)
            if first:
                vstep(lambda e: e.memset(x13[:, :, 0:1], 0.0),
                      writes=(B["X1R"],))
            if last_t:
                vstep(lambda e: e.memset(x13[:, :, WD - 1:WD], 0.0),
                      writes=(B["X1R"],))
            phase_dump("x1", x1r, B["X1R"], [128, NC_D * WD], F32)

            for c in range(NC_D):
                def emit_conv(e, c=c):
                    e.tensor_scalar(mid[:, c * T:(c + 1) * T],
                                    x1r[:, c * WD:c * WD + 512],
                                    convw_s[:, 3 * c:3 * c + 1], None, ALU.mult)
                    e.scalar_tensor_tensor(mid[:, c * T:(c + 1) * T],
                                           x1r[:, c * WD + 1:c * WD + 513],
                                           convw_s[:, 3 * c + 1:3 * c + 2],
                                           mid[:, c * T:(c + 1) * T],
                                           ALU.mult, ALU.add)
                    return e.scalar_tensor_tensor(
                        mid[:, c * T:(c + 1) * T],
                        x1r[:, c * WD + 2:c * WD + 514],
                        convw_s[:, 3 * c + 2:3 * c + 3],
                        mid[:, c * T:(c + 1) * T], ALU.mult, ALU.add)
                vstep(emit_conv, reads=(B["X1R"], B["CONST"]),
                      writes=(B["MID"],))
                def emit_silu(e, c=c):
                    return e.activation(xcbf[:, c * T:(c + 1) * T],
                                        mid[:, c * T:(c + 1) * T], AF.Silu)
                astep(emit_silu, reads=(B["MID"],), writes=(B["XCBF"],))
            phase_dump("xc", xcbf, B["XCBF"], [128, NC_D * T], BF16)

            VPS = (SM, SM, BCA, BCA, BCB, BCB, MMX, MMX)
            VPSB = (B["SM0"], B["SM1"], B["BCA0"], B["BCA1"],
                    B["BCB0"], B["BCB1"], B["MMX0"], B["MMX1"])
            for c in range(NC_D):
                acc = VPS[c][:, (c % 2) * 512:(c % 2) * 512 + 512]
                def emit_mm(e, c=c, acc=acc):
                    last = None
                    for k in range(NC_D):
                        last = e.matmul(
                            acc,
                            v1t_bf[:, k * D + c * 128:k * D + c * 128 + 128],
                            h1bf[:, k * WD + 1:k * WD + 513],
                            start=(k == 0), stop=(k == NC_D - 1))
                    return last
                tstep(emit_mm, reads=(B["H1BF"], B["RES"]), writes=(VPSB[c],))
            for c in range(NC_D):
                acc = VPS[c][:, (c % 2) * 512:(c % 2) * 512 + 512]
                def emit_ev(e, c=c, acc=acc):
                    return e.activation(h1bf[:, c * WD:c * WD + 512],
                                        acc, AF.Silu,
                                        bias=v1b_s[:, c:c + 1])
                astep(emit_ev, reads=(VPSB[c], B["CONST"]),
                      writes=(B["H1BF"],))
            phase_dump("v", h1bf, B["H1BF"], [128, NC_D * WD], BF16)

            def emit_umm(e):
                for k in range(NC_D):
                    e.matmul(BCB[0:64, 0:512],
                             bmt_bf[:, k * DST:(k + 1) * DST],
                             xcbf[:, k * T:(k + 1) * T],
                             start=(k == 0), stop=(k == NC_D - 1))
                last = None
                for k in range(NC_D):
                    last = e.matmul(BCB[64:128, 0:512],
                                    bmt_bf[:, k * DST:(k + 1) * DST],
                                    xcbf[:, k * T:(k + 1) * T],
                                    start=(k == 0), stop=(k == NC_D - 1))
                return last
            tstep(emit_umm, reads=(B["XCBF"], B["RES"]), writes=(B["BCB0"],))
            up_lo = upad[0:64, :].rearrange("p (c w) -> p c w", w=64)[:, :, 0:32]
            up_hi = upad[64:128, :].rearrange("p (c w) -> p c w", w=64)[:, :, 1:33]
            src_lo = BCB[0:64, 0:512].rearrange("p (c w) -> p c w", w=32)
            src_hi = BCB[64:128, 0:512].rearrange("p (c w) -> p c w", w=32)
            vstep(vv(lambda e: e.tensor_copy(up_lo, src_lo),
                     lambda e: e.tensor_copy(up_hi, src_hi)),
                  reads=(B["BCB0"],), writes=(B["UPAD"],))

            def emit_shifts(e):
                last = None
                for j in range(NPAIR):
                    lhs = ppair[:, j * DST:(j + 1) * DST]
                    e.matmul(MMX[0:64, 2 * j:512], lhs,
                             upad[:, 0:512 - 2 * j],
                             start=(j == 0), stop=False, skip_group_check=True)
                    last = e.matmul(MMX[0:64, 512:1024], lhs,
                                    upad[:, 512 - 2 * j:1024 - 2 * j],
                                    start=(j == 0), stop=False,
                                    skip_group_check=True)
                return last
            tstep(emit_shifts, reads=(B["UPAD"], B["PPAIR"]),
                  writes=(B["MMX0"], B["MMX1"]))

            esrc = MMX[0:64, :].rearrange("p (c w) -> p c w", w=64)[:, :, 31:32]
            edst = esb[:, :].rearrange("p (c b) -> p c b", b=1)
            vstep(lambda e: e.tensor_copy(edst, esrc),
                  reads=(B["MMX0"], B["MMX1"]), writes=(B["ESB"],))

            if first:
                vstep(lambda e: e.memset(hs[:, 0:1], 0.0), writes=(B["HS"],))
            else:
                vstep(lambda e: e.tensor_copy(hs[:, 0:1],
                                              hs[:, NCHK:NCHK + 1]),
                      reads=(B["HS"],), writes=(B["HS"],))

            def emit_hsmm(e):
                for k in range(NCHK):
                    e.matmul(BCB[0:DST, k + 1:NCHK + 1],
                             qpbf[:, k * DST:(k + 1) * DST],
                             esb[:, 0:NCHK - k],
                             start=(k == 0), stop=False,
                             skip_group_check=True)
                last = None
                for c2 in range(1, NCHK + 1):
                    last = e.matmul(BCB[0:DST, c2:c2 + 1],
                                    qpbf[:, c2 * DST:(c2 + 1) * DST],
                                    hs[:, 0:1], start=False,
                                    stop=(c2 == NCHK), skip_group_check=True)
                return last
            tstep(emit_hsmm, reads=(B["ESB"], B["HS"], B["PBF"]),
                  writes=(B["BCB0"],))
            vstep(lambda e: e.tensor_copy(hs[:, 1:NCHK + 1],
                                          BCB[0:DST, 1:NCHK + 1]),
                  reads=(B["BCB0"],), writes=(B["HS"],))

            def emit_g(e):
                last = None
                for q in range(L):
                    lhs = pbf[:, (q + 1) * DST:(q + 2) * DST]
                    e.matmul(MMX[0:64, q:512:64], lhs, hs[:, 0:8],
                             start=False, stop=False, skip_group_check=True)
                    last = e.matmul(MMX[0:64, 512 + q:1024:64], lhs,
                                    hs[:, 8:16], start=False,
                                    stop=(q == L - 1), skip_group_check=True)
                return last
            tstep(emit_g, reads=(B["HS"], B["PBF"]),
                  writes=(B["MMX0"], B["MMX1"]))
            hsrc = MMX[0:64, :].rearrange("p (c w) -> p c w", w=64)[:, :, 0:32]
            hdst = hbf[:, :].rearrange("p (c w) -> p c w", w=32)
            vstep(lambda e: e.tensor_copy(hdst, hsrc),
                  reads=(B["MMX0"], B["MMX1"]), writes=(B["HBF"],))

            for c in range(NC_D):
                rot_ap = MMX[:, 0:512] if (c % 2 == 0) else MMX[:, 512:1024]
                rotb = B["MMX0"] if (c % 2 == 0) else B["MMX1"]
                def emit_y(e, c=c, rot_ap=rot_ap):
                    return e.matmul(rot_ap, cmt_bf[:, c * 128:(c + 1) * 128],
                                    hbf[:], start=True, stop=True)
                tstep(emit_y, reads=(B["HBF"], B["RES"]), writes=(rotb,))
                def emit_gate(e, c=c, rot_ap=rot_ap):
                    return e.tensor_tensor(xcbf[:, c * T:(c + 1) * T], rot_ap,
                                           h1bf[:, c * WD:c * WD + 512],
                                           ALU.mult)
                vstep(emit_gate, reads=(rotb, B["H1BF"]), writes=(B["XCBF"],))

            for c in range(NC_D):
                rot_ap = MMX[:, 0:512] if (c % 2 == 0) else MMX[:, 512:1024]
                rotb = B["MMX0"] if (c % 2 == 0) else B["MMX1"]
                def emit_mm(e, c=c, rot_ap=rot_ap):
                    last = None
                    for k in range(NC_D):
                        last = e.matmul(
                            rot_ap,
                            w2t_bf[:, k * D + c * 128:k * D + c * 128 + 128],
                            xcbf[:, k * T:(k + 1) * T],
                            start=(k == 0), stop=(k == NC_D - 1))
                    return last
                tstep(emit_mm, reads=(B["XCBF"], B["RES"]), writes=(rotb,))
                def emit_ev(e, c=c, rot_ap=rot_ap):
                    return e.scalar_tensor_tensor(
                        mid[:, c * T:(c + 1) * T], rot_ap, w2b_s[:, c:c + 1],
                        xnbf[:, c * WD + 1:c * WD + 513], ALU.add, ALU.add)
                vstep(emit_ev, reads=(rotb, B["CONST"], B["XNBF"]),
                      writes=(B["MID"],))
                def emit_mc(e, c=c):
                    return e.activation(h1bf[:, c * WD:c * WD + 512],
                                        mid[:, c * T:(c + 1) * T],
                                        AF.Identity)
                astep(emit_mc, reads=(B["MID"],), writes=(B["H1BF"],))

            if tile_idx + 1 < len(all_tiles):
                emit_xload(*all_tiles[tile_idx + 1])

            emit_ln(mid[:], B["MID"], x1r[:], B["X1R"], T,
                    gb=(g2_s, b2ln_s), out_bf_ap=h1bf[:], outbf_buf=B["H1BF"],
                    skip_cast=True, cast_t=h1bf, cast_buf=B["H1BF"],
                    cast_stride=WD, sq_t=xcbf, sq_buf=B["XCBF"], sq_stride=T)


            for m in range(16):
                f1slot, f1buf = ring_f1.use()
                rot_ap = MMX[:, 0:512] if (m % 2 == 0) else MMX[:, 512:1024]
                rotb = B["MMX0"] if (m % 2 == 0) else B["MMX1"]
                def emit_mm(e, f1slot=f1slot, rot_ap=rot_ap):
                    last = None
                    for k in range(NC_D):
                        last = e.matmul(
                            rot_ap, f1slot[:, k * 128:(k + 1) * 128],
                            h1bf[:, k * WD:k * WD + 512],
                            start=(k == 0), stop=(k == NC_D - 1))
                    return last
                tstep(emit_mm, reads=(B["H1BF"], f1buf), writes=(rotb,))
                ring_f1.fill_next()
                def emit_ev(e, m=m, rot_ap=rot_ap):
                    return e.activation(gelbf[:, m * T:(m + 1) * T], rot_ap,
                                        AF.Gelu, bias=ff1b_s[:, m:m + 1])
                astep(emit_ev, reads=(rotb, B["CONST"]), writes=(B["X1R"],))

            PS4 = (SM, BCA, BCB, MMX)
            PSB = (B["SM0"], B["SM1"], B["BCA0"], B["BCA1"],
                   B["BCB0"], B["BCB1"], B["MMX0"], B["MMX1"])
            for k in range(16):
                f2slot, f2buf = ring_f2.use()
                def emit_ff2(e, k=k, f2slot=f2slot):
                    last = None
                    for o in range(8):
                        for th in range(2):
                            r = o * 2 + th
                            acc = PS4[r // 4]
                            col = (r % 4) * 256
                            last = e.matmul(
                                acc[:, col:col + 256],
                                f2slot[:, o * 128:(o + 1) * 128],
                                gelbf[:, k * T + th * QH:k * T + th * QH + QH],
                                start=(k == 0 and r % 2 == 0), stop=False,
                                skip_group_check=True)
                    return last
                tstep(emit_ff2, reads=(f2buf, B["X1R"]), writes=PSB)
                ring_f2.fill_next()
            for r in list(range(12, 16)) + list(range(12)):
                    o = r // 2
                    th = r % 2
                    acc = PS4[r // 4]
                    col = (r % 4) * 256
                    dst = mid[:, o * T + th * QH:o * T + th * QH + QH]
                    bankbuf = PSB[(r // 4) * 2 + (r % 4) // 2]
                    def emit_ev(e, acc=acc, col=col, dst=dst, o=o):
                        return e.scalar_tensor_tensor(
                            dst, acc[:, col:col + 256], ff2b_s[:, o:o + 1],
                            dst, ALU.add, ALU.add)
                    vstep(emit_ev, reads=(bankbuf, B["CONST"], B["MID"]),
                          writes=(B["MID"],))

            def emit_am(e, tile_idx=tile_idx):
                last = None
                for c in range(NC_D):
                    if tile_idx == 0:
                        last = e.scalar_tensor_tensor(
                            ffn[:, c * T:(c + 1) * T],
                            mid[:, c * T:(c + 1) * T], qb2_s[:, c:c + 1],
                            ffn[:, c * T:(c + 1) * T], ALU.add, ALU.add)
                    else:
                        last = e.tensor_tensor(ffn[:, c * T:(c + 1) * T],
                                               ffn[:, c * T:(c + 1) * T],
                                               mid[:, c * T:(c + 1) * T],
                                               ALU.add)
                return last
            vstep(emit_am, reads=(B["MID"], B["CONST"]) + tuple(FFNq),
                  writes=tuple(FFNq))

            def q_w1(c):
                w1slot, w1buf = ring_w1.use()
                rot_ap = MMX[:, 0:512] if (c % 2 == 0) else MMX[:, 512:1024]
                rotb = B["MMX0"] if (c % 2 == 0) else B["MMX1"]
                def emit_w1(e, w1slot=w1slot, rot_ap=rot_ap):
                    last = None
                    for k in range(NC_D):
                        last = e.matmul(
                            rot_ap, w1slot[:, k * 128:(k + 1) * 128],
                            xnbf[:, k * WD + 1:k * WD + 513],
                            start=(k == 0), stop=(k == NC_D - 1))
                    return last
                tstep(emit_w1, reads=(B["XNBF"], w1buf), writes=(rotb,))
                ring_w1.fill_next()
                hs3 = c % 3
                sr_ap = mr[:, (c % 2) * T:(c % 2) * T + T]
                cr_ap = mr[:, 2 * T:3 * T] if (c % 2 == 0) else kr[:]
                srb = MRq[c % 2]
                crb = MRq[2] if (c % 2 == 0) else KRq[0]
                def emit_ev(e, rot_ap=rot_ap, hs3=hs3, c=c,
                            sr_ap=sr_ap, cr_ap=cr_ap):
                    e.activation(hidr[:, hs3 * T:(hs3 + 1) * T],
                                 rot_ap, AF.Identity,
                                 bias=qb1_s[:, c:c + 1])
                    e.activation(sr_ap, rot_ap, AF.Sin,
                                 bias=qx_s[:, c:c + 1], scale=0.1)
                    return e.activation(cr_ap, rot_ap, AF.Sin,
                                        bias=qx_s[:, 32 + c:32 + c + 1],
                                        scale=0.1)
                astep(emit_ev, reads=(rotb, B["CONST"], B["QX"]),
                      writes=(HIDq[hs3], srb, crb))

            def q_mix(c):
                hs3 = c % 3
                hid_ap = hidr[:, hs3 * T:(hs3 + 1) * T]
                sr_ap = mr[:, (c % 2) * T:(c % 2) * T + T]
                cr_ap = mr[:, 2 * T:3 * T] if (c % 2 == 0) else kr[:]
                srb = MRq[c % 2]
                crb = MRq[2] if (c % 2 == 0) else KRq[0]
                def emit_mix(e, hid_ap=hid_ap, sr_ap=sr_ap, cr_ap=cr_ap, c=c):
                    e.scalar_tensor_tensor(hid_ap, cr_ap,
                                           qx_s[:, 64 + c:64 + c + 1],
                                           hid_ap, ALU.mult, ALU.add)
                    return e.scalar_tensor_tensor(hid_ap, sr_ap,
                                                  qx_s[:, 96 + c:96 + c + 1],
                                                  hid_ap, ALU.mult, ALU.add)
                vstep(emit_mix, reads=(srb, crb, HIDq[hs3], B["QX"]),
                      writes=(HIDq[hs3],))

            def q_relu(c):
                hs3 = c % 3
                rs = c % 6
                hid_ap = hidr[:, hs3 * T:(hs3 + 1) * T]
                r_ap = rbf[:, rs * T:(rs + 1) * T]
                astep(lambda e, hid_ap=hid_ap, r_ap=r_ap:
                      e.activation(r_ap, hid_ap, AF.Relu),
                      reads=(HIDq[hs3],),
                      writes=(RBFq[rs], B["XCBF"]))

            def q_w2group(g):
                slots = []
                for kk in range(4):
                    w2slot, w2buf = ring_w2.use()
                    slots.append((w2slot, w2buf))
                for o in range(8):
                    acc_ap = BCA[:, 0:512] if (o % 2 == 0) else BCA[:, 512:1024]
                    accb = B["BCA0"] if (o % 2 == 0) else B["BCA1"]
                    def emit_grp(e, g=g, o=o, acc_ap=acc_ap, slots=slots):
                        last = None
                        for kk in range(4):
                            c2 = 4 * g + kk
                            last = e.matmul(
                                acc_ap,
                                slots[kk][0][:, o * 128:(o + 1) * 128],
                                rbf[:, (c2 % 6) * T:(c2 % 6 + 1) * T],
                                start=(kk == 0), stop=(kk == 3))
                        return last
                    rb = tuple(RBFq[(4 * g + kk) % 6] for kk in range(4))
                    tstep(emit_grp,
                          reads=rb + (B["XCBF"],) + tuple(s[1] for s in slots),
                          writes=(accb,))
                    dst = ffn[:, o * T:(o + 1) * T]
                    mdst = mid[:, o * T:(o + 1) * T]
                    if g == 7:
                        def emit_add(e, acc_ap=acc_ap, dst=dst, mdst=mdst):
                            return e.scalar_tensor_tensor(
                                mdst, dst, 1.0, acc_ap, ALU.mult, ALU.add)
                        vstep(emit_add, reads=(accb, FFNq[o // 2]),
                              writes=(B["MID"],))
                    else:
                        def emit_add(e, acc_ap=acc_ap, dst=dst):
                            return e.tensor_tensor(dst, dst, acc_ap, ALU.add)
                        vstep(emit_add, reads=(accb, FFNq[o // 2]),
                              writes=(FFNq[o // 2],))
                for kk in range(4):
                    ring_w2.fill_next()

            if tile_idx + 1 < len(all_tiles):
                nstg = ln_stage_list(*all_tiles[tile_idx + 1])
            else:
                nstg = []
            hookmap = {2: 0, 3: 1, 8: 2, 9: 3, 10: 4, 11: 5, 12: 6, 13: 7,
                       14: 8, 15: 9, 24: 10, 25: 11, 27: 12, 28: 13}
            for c in range(32):
                if nstg and c in hookmap:
                    nstg[hookmap[c]]()
                q_w1(c)
                if c >= 1:
                    q_mix(c - 1)
                    q_relu(c - 1)
                if c >= 5 and (c - 5) % 4 == 0:
                    q_w2group((c - 5) // 4)
            q_mix(31)
            q_relu(31)
            if nstg:
                nstg[14]()
                nstg[15]()
            q_w2group(7)

            odst = out_d[seq].rearrange("c p t -> p c t")[:, :, t0:t0 + T]
            osrc = mid[:].rearrange("p (c w) -> p c w", w=T)
            gdma(lambda e, d=odst, s=osrc: e.dma_start(d, s),
                 reads=(B["MID"],), writes=())

            if dbg_names and not dbg_done[0]:
                dbg_done[0] = True
                cat = {"xa": (xa, B["XA"], [128, NC_D * WD], F32),
                       "xnbf": (xnbf, B["XNBF"], [128, NC_D * WD], BF16),
                       "h1bf": (h1bf, B["H1BF"], [128, NC_D * WD], BF16),
                       "x1r": (x1r, B["X1R"], [128, NC_D * WD], F32),
                       "xcbf": (xcbf, B["XCBF"], [128, NC_D * T], BF16),
                       "mid": (mid, B["MID"], [128, NC_D * T], F32),
                       "upad": (upad, B["UPAD"], [128, 1024], BF16),
                       "hs": (hs, B["HS"], [DST, NCHK + 1], BF16),
                       "esb": (esb, B["ESB"], [DST, NCHK], F32),
                       "hbf": (hbf, B["HBF"], [DST, T], BF16),
                       "qthm": (qthm_s, B["QTHM"], [128, 256], F32),
                       "qths": (qth_s, B["CONST"], [128, 32], F32),
                       "g1s": (g1_s, B["CONST"], [128, 8], F32),
                       "pbf": (pbf, B["PBF"], [64, 33 * 64], BF16),
                       "qc": (qc_s, B["QC"], [128, 256], F32),
                       "qx": (qx_s, B["QX"], [128, 128], F32),
                       "ffn": (ffn, FFNq[0], [128, NC_D * T], F32),
                       "rbf": (rbf, RBFq[0], [128, 6 * T], BF16),
                       "hidr": (hidr, HIDq[0], [128, 3 * T], F32)}
                for nm in dbg_names:
                    if nm not in cat:
                        continue
                    tsr, bb, shp, dt = cat[nm]
                    dd = nc.dram_tensor("dbg_" + nm, shp, dt,
                                        kind="ExternalOutput").ap()
                    dbg_specs.append(nm)
                    gdma(lambda e, d=dd, s=tsr: e.dma_start(d, s[:]),
                         reads=(bb,), writes=())

    sems = {k: nc.alloc_semaphore("sem" + k) for k in p.cnt.keys()}
    qkeys = [k for k in p.cnt if k.startswith("Q") or k.startswith("qw")
             or k.startswith("qf") or k == "QC"]
    gkeys = [k for k in p.cnt if k in ("GO", "GX", "GF", "GR")
             or k.startswith("GS")]

    def runner(skey):
        steps = p.streams[skey]
        def run(eng):
            for waits, emit, semkey, inc in steps:
                for (k, v) in waits:
                    eng.wait_ge(sems[k], v)
                last = emit(eng)
                last.then_inc(sems[semkey], inc)
            if skey == "G":
                for k in gkeys:
                    eng.wait_ge(sems[k], p.cnt[k])
            if skey == "Q":
                for k in qkeys:
                    eng.wait_ge(sems[k], p.cnt[k])
        return run

    with nc.Block() as blk:
        blk.tensor(runner("T"))
        blk.vector(runner("V"))
        blk.scalar(runner("A"))
        blk.sync(runner("Q"))
        blk.gpsimd(runner("G"))

    return nc, dbg_specs


def _prep_weights(iv):
    def pad256(a):
        z = np.zeros((128, 256), np.float32)
        z[:, :a.shape[1]] = a
        return z

    def t(a):
        return np.ascontiguousarray(
            np.asarray(a).T.astype(np.float32, copy=False))

    def vec(a, k):
        return np.ascontiguousarray(
            np.asarray(a).astype(np.float32, copy=False).reshape(k, 128).T)

    conv = np.asarray(iv["L0_attn_conv_w"]).astype(np.float32, copy=False)
    convw = np.ascontiguousarray(
        conv[:, 0, :].reshape(8, 128, 3).transpose(1, 0, 2).reshape(128, 24))
    return {
        "w1t": t(iv["L0_attn_w1_w"]), "v1t": t(iv["L0_attn_v1_w"]),
        "w2t": t(iv["L0_attn_w2_w"]),
        "bmt": t(iv["L0_attn_Bm"]), "cmt": t(iv["L0_attn_Cm"]),
        "amat": np.ascontiguousarray(
            np.asarray(iv["L0_attn_A"]).astype(np.float32, copy=False)),
        "ff1t": t(iv["L0_attn_ff1_w"]), "ff2t": t(iv["L0_attn_ff2_w"]),
        "W1": np.ascontiguousarray(
            np.asarray(iv["L0_ffn_W1"]).astype(np.float32, copy=False)),
        "W2": np.ascontiguousarray(
            np.asarray(iv["L0_ffn_W2"]).astype(np.float32, copy=False)),
        "convw": convw,
        "g1": vec(iv["L0_attn_ln1_g"], 8), "b1ln": vec(iv["L0_attn_ln1_b"], 8),
        "g2": vec(iv["L0_attn_ln2_g"], 8), "b2ln": vec(iv["L0_attn_ln2_b"], 8),
        "w1b": vec(iv["L0_attn_w1_b"], 8), "v1b": vec(iv["L0_attn_v1_b"], 8),
        "w2b": vec(iv["L0_attn_w2_b"], 8),
        "ff1b": vec(iv["L0_attn_ff1_b"], 16),
        "ff2b": vec(iv["L0_attn_ff2_b"], 8),
        "qb1": vec(iv["L0_ffn_b1"], 32), "qb2": vec(iv["L0_ffn_b2"], 8),
        "qth": vec(iv["L0_ffn_quantum_phases"], 32),
        "qwr": vec(iv["L0_ffn_quantum_weights_real"], 32),
        "qwi": vec(iv["L0_ffn_quantum_weights_imag"], 32),
        "qb1p_in": pad256(vec(iv["L0_ffn_b1"], 32)),
        "qthp_in": pad256(vec(iv["L0_ffn_quantum_phases"], 32)),
        "qwrp_in": pad256(vec(iv["L0_ffn_quantum_weights_real"], 32)),
        "qwip_in": pad256(vec(iv["L0_ffn_quantum_weights_imag"], 32)),
    }


def kernel(**inputs):
    NCORES = 8
    x = np.asarray(inputs["L0_attn_x"]).astype(np.float32, copy=False)
    Bt, S, Dm = x.shape
    per = Bt // NCORES
    wmap = _prep_weights(inputs)
    nc, _ = build_program(per, S)
    in_maps = []
    for c in range(NCORES):
        xs = np.ascontiguousarray(
            x[c * per:(c + 1) * per].reshape(per, S, NC_D, 128)
            .transpose(0, 2, 3, 1))
        m = {"x": xs}
        m.update(wmap)
        in_maps.append(m)
    res = run_bass_kernel_spmd(nc, in_maps, core_ids=list(range(NCORES)))
    outs = []
    for c in range(NCORES):
        o = np.asarray(res.results[c]["out"])
        outs.append(o.transpose(0, 3, 1, 2).reshape(per, S, Dm))
    return np.concatenate(outs, axis=0)
